# version 11
# baseline (speedup 1.0000x reference)
import numpy as np
import ml_dtypes
import concourse.bass as bass
import concourse.mybir as mybir
from concourse.bass_utils import run_bass_kernel_spmd

F32 = mybir.dt.float32
BF16 = mybir.dt.bfloat16
AF = mybir.ActivationFunctionType
ALU = mybir.AluOpType
AX = mybir.AxisListType.X
bfdt = ml_dtypes.bfloat16

D = 4096
H = 32
KVH = 8
HD = 128
GROUP = 4
SHIFT = 20.0
NEGBIG = -1e30
MASKV = -1e38
EPS = 1e-5
SC_ATT = 1.0 / float(np.sqrt(128.0))
SC_RT = 1.0 / float(np.sqrt(48.0))

_DBG = False
_STOP = 99
_CACHE = {}



def _slot_to_block(half):
    if half == 0:
        return [4, 5, 6, 7, 8, 9, 10, 11, 0, 1, 2, 3, 12, 13, 14, 15]
    return [-1, -1, -1, -1, 0, 1, 2, 3, 4, 5, 6, 7, 8, 9, 10, 11]


def _own_blocks(half):
    return _slot_to_block(half)[8:16]


def _perm_qkv_rows():
    perm = []
    for h in range(H):
        b0 = h * HD
        perm += [b0 + 2 * i for i in range(64)] + [b0 + 2 * i + 1 for i in range(64)]
    for g in range(KVH):
        b0 = 4096 + g * HD
        perm += [b0 + 2 * i for i in range(64)] + [b0 + 2 * i + 1 for i in range(64)]
    perm += list(range(5120, 6144))
    return np.array(perm)


def _rope_tables(positions):
    inv = 1.0 / (10000.0 ** (np.arange(0, HD, 2, dtype=np.float64) / HD))
    ang = positions[None, :].astype(np.float64) * inv[:, None]
    C = np.concatenate([np.cos(ang), np.cos(ang)], 0)
    Sg = np.concatenate([-np.sin(ang), np.sin(ang)], 0)
    return C.astype(bfdt), Sg.astype(bfdt)


def _tri_masks():
    ms = np.zeros((4, 128, 512), np.float32)
    kk = np.arange(128)[:, None]
    qq = np.arange(128)[None, :]
    blk = np.where(kk > qq, MASKV, 0.0)
    for j in range(4):
        ms[j, :, : j * 128] = MASKV
        ms[j, :, j * 128:(j + 1) * 128] = blk
    return ms.astype(bfdt)


def _bias_tbl(half):
    t = np.full((128, 24), -SHIFT, np.float32)
    if half == 0:
        t[:, 0:4] = NEGBIG
    else:
        t[:, 8:12] = NEGBIG
    return t




class Alloc:

    BASE = 16512
    CAP = 212600

    def __init__(self, nc, placement=None):
        self.nc = nc
        self.placement = placement
        self.reqs = []

    def sb(self, name, shape, dtype, lo, hi):
        per = int(np.prod(shape[1:])) * mybir.dt.size(dtype)
        size = (per + 31) // 32 * 32
        self.reqs.append((name, size, lo, hi))
        if self.placement is None:
            return self.nc.alloc_sbuf_tensor_at(
                name, list(shape), dtype, offset=self.BASE)
        off = self.placement[name]
        assert off + size <= self.CAP, (name, off, size)
        return self.nc.alloc_sbuf_tensor_at(
            name, list(shape), dtype, offset=self.BASE + off)

    @staticmethod
    def _try_solve(reqs):
        placed = []
        out = {}
        for name, size, lo, hi in reqs:
            if name in Alloc.PIN:
                off = Alloc.PIN[name]
                placed.append((off, size, lo, hi))
                out[name] = off
                continue
            best = None
            for off in sorted({0} | {o + s for (o, s, l, h) in placed}):
                if off + size > Alloc.CAP:
                    continue
                if all(hi < l or h < lo or off + size <= o or o + s <= off
                       for (o, s, l, h) in placed):
                    best = off
                    break
            if best is None:
                return None
            placed.append((best, size, lo, hi))
            out[name] = best
        return out

    PIN = {"kbuf": 0, "vbuf": 32768, "x12": 0,
           "xcb": 65536, "abuf": 65536, "x3b": 65536}

    @staticmethod
    def solve(reqs, cap):
        orders = [
            sorted(reqs, key=lambda r: (-(r[3] - r[2]), -r[1])),
            sorted(reqs, key=lambda r: -r[1]),
            sorted(reqs, key=lambda r: (-r[1], -(r[3] - r[2]))),
            sorted(reqs, key=lambda r: (r[2], -r[1])),
        ]
        for o in orders:
            o = ([r for r in o if r[0] in Alloc.PIN]
                 + [r for r in o if r[0] not in Alloc.PIN])
            res = Alloc._try_solve(o)
            if res is not None:
                return res
        raise MemoryError("SBUF solve failed under all orderings")


class Sched:
    ENGS = ("tensor", "vector", "scalar", "gpsimd", "sync")

    def __init__(self, semnames):
        self.q = {e: [] for e in self.ENGS}
        self.cnt = {s: 0 for s in semnames}
        self.last_wait = {e: {s: 0 for s in semnames} for e in self.ENGS}
        self.sems = None

    stopped = False

    def stop_mark(self, n):
        if n >= _STOP:
            self.stopped = True

    def emit(self, eng, fn):
        if self.stopped:
            return
        self.q[eng].append(fn)

    def wait(self, eng, sem, val):
        if self.stopped:
            return
        if val <= 0 or val <= self.last_wait[eng][sem]:
            return
        self.last_wait[eng][sem] = val
        sel = self

        def f(e, _s=sem, _v=val):
            e.wait_ge(sel.sems[_s], _v)
        self.q[eng].append(f)

    def bump(self, sem, by=1):
        self.cnt[sem] += by
        return self.cnt[sem]


_PLACEMENT = {}


def _build_program(dbg=False):
    if _PLACEMENT.get("key") != dbg:
        _PLACEMENT.pop("map", None)
        rec = _build_program_inner(dbg, record=True)
        _PLACEMENT["map"] = Alloc.solve(rec, Alloc.CAP)
        _PLACEMENT["key"] = dbg
    return _build_program_inner(dbg, record=False)


def _build_program_inner(dbg=False, record=False):
    nc = bass.Bass("TRN2", target_bir_lowering=False, debug=False,
                   detect_race_conditions=False)

    def din(name, shape, dtype=F32):
        return nc.dram_tensor(name, list(shape), dtype, kind="ExternalInput").ap()

    xT = din("xT", [D, 2048])
    xsw = din("xsw", [2, 2, 128, 32 * 512])
    wkT = din("wkT", [KVH, 128, 4096])
    wvT = din("wvT", [4, 128, 8192])
    wqT = din("wqT", [H, 128, 4096])
    woT = din("woT", [32, 128, 4096])
    bq_r = din("bq_r", [1, 4096])
    bk_r = din("bk_r", [1, 1024])
    bv_r = din("bv_r", [1, 1024])
    bo_r = din("bo_r", [1, 4096])
    ck_t = din("ck_t", [128, 2048], BF16)
    sk_t = din("sk_t", [128, 2048], BF16)
    tri_m = din("tri_m", [128, 2048], BF16)
    ident = din("ident", [128, 128], BF16)
    identf = din("identf", [128, 128])
    ones_c = din("ones_c", [128, 1], BF16)
    ones_r = din("ones_r", [1, 128], BF16)
    ones512 = din("ones512", [1, 512], BF16)
    ones_rf = din("ones_rf", [1, 128])
    btbl = din("btbl", [128, 24])
    fq = din("fq", [128, 2048])
    fr = din("fr", [64, 48])
    fak = din("fak", [64, 1024])
    favT = din("favT", [8, 128, 4096])
    fvd = din("fvd", [128, 8192])
    fvu = din("fvu", [128, 8192])
    fu1 = din("fu1", [8, 128, 4096])
    fbu1 = din("fbu1", [128, 8])
    fu2 = din("fu2", [128, 8])
    fbu2 = din("fbu2", [1, 1])
    fgw = din("fgw", [1, 1])
    fgb = din("fgb", [1, 1])
    flr = din("flr", [128, 1024])
    crow = din("crow", [1, 8])

    out_d = nc.dram_tensor("out", [D, 1024], F32, kind="ExternalOutput").ap()
    qrope_d = nc.dram_tensor("qrope", [D, 1024], BF16, kind="Internal").ap()
    cc_in = nc.dram_tensor("cc_in", [2, 64], F32, kind="Internal").ap()
    cc_out = nc.dram_tensor("cc_out", [2, 64], F32, kind="Internal").ap()

    dbg_t = {}
    if dbg:
        for name, shape, dty in [
                ("d_k", [128, KVH * 2048], BF16), ("d_v", [128, 16 * 1024], BF16),
                ("d_at", [128, 32 * 1024], BF16), ("d_x2", [128, 32 * 1024], BF16),
                ("d_em", [128, 8 * 1024], BF16), ("d_rows", [4, 1024], F32),
                ("d_gc", [128, 8], F32), ("d_qt", [128, 1024], BF16),
                ("d_avp", [128, 2048], BF16), ("d_x3", [128, 32 * 1024], BF16)]:
            dbg_t[name] = nc.dram_tensor(name, shape, dty, kind="ExternalOutput").ap()

    AL = Alloc(nc, placement=None if record else _PLACEMENT["map"])
    sb = AL.sb

    def sbd(name, shape, dtype, lo, hi):
        return AL.sb(name, shape, dtype, lo, hi + (2 if dbg else 0))

    P = 99
    kbuf = sb("kbuf", [128, KVH * 2048], BF16, 1, 3)
    vbuf = sb("vbuf", [128, 16 * 1024], BF16, 1, 3)
    x12 = sb("x12", [128, 32 * 1024], BF16, 4, 9)
    xcb = sb("xcb", [128, 32 * 1024], BF16, 1, 2)
    abuf = sb("abuf", [128, 32 * 1024], BF16, 3, 4)
    x3b = sb("x3b", [128, 32 * 1024], BF16, 9, 10)
    wop_s = [sb("wop0", [128, 32 * 128], BF16, 2, 4),
             sb("wop1", [128, 32 * 128], BF16, 4, 4)]
    wvd_s = sb("wvd_s", [128, 32 * 256], BF16, 3, 7)
    avpan_s = [sb(f"avpan{i}", [128, 4096], BF16, 3, 5) for i in range(2)]
    avp_s = sbd("avp_s", [128, 8 * 256], BF16, 5, 8)

    tri_s = sb("tri_s", [128, 4 * 512], BF16, 2, 3)
    id_s = sb("id_s", [128, 128], BF16, 0, 8)
    idf_s = sb("idf_s", [128, 128], F32, 0, 8)
    onc_s = sb("onc_s", [128, 1], BF16, 0, 9)
    onr_s = sb("onr_s", [1, 128], BF16, 0, 5)
    onrf_s = sb("onrf_s", [1, 128], F32, 0, 10)
    btbl_s = sb("btbl_s", [128, 24], F32, 0, 3)
    bqb_s = sb("bqb_s", [1, 4096], BF16, 2, 2)
    bkb_s = sb("bkb_s", [1, 1024], BF16, 0, 1)
    bob_s = sb("bob_s", [1, 4096], BF16, 2, 4)
    on5_s = sb("on5_s", [1, 512], BF16, 0, 7)
    bu1_s = sb("bu1_s", [128, 8], F32, 0, 7)
    bvb_s = sb("bvb_s", [1, 1024], BF16, 0, 1)
    wu2_s = sb("wu2_s", [128, 8], BF16, 0, 7)
    gw_s = sb("gw_s", [1, 1], F32, 0, 8)
    gb_s = sb("gb_s", [1, 1], F32, 0, 8)
    lv_s = sb("lv_s", [1, 1024], F32, 5, 8)
    gsum_s = sb("gsum_s", [2, 64], F32, 5, 8)
    bu2b_s = sb("bu2b_s", [1, 1], BF16, 0, 7)
    cst_s = sb("cst_s", [1, 8], F32, 0, 10)

    ck_lo = sb("ck_lo", [128, 1024], BF16, 1, 1)
    sk_lo = sb("sk_lo", [128, 1024], BF16, 1, 1)
    ck_hi = sb("ck_hi", [128, 1024], BF16, 1, 2)
    sk_hi = sb("sk_hi", [128, 1024], BF16, 1, 2)
    kp_s = [sb(f"kp{i}", [128, 4096], BF16, 1, 2) for i in range(2)]
    vp_s = [sb(f"vp{i}", [128, 32 * 256], BF16, 1, 1) for i in range(2)]
    ktmp_s = [sb(f"ktmp{i}", [128, 512], BF16, 1, 2) for i in range(2)]
    kswp_s = [sb(f"kswp{i}", [128, 512], BF16, 1, 2) for i in range(2)]
    qro_s = [sb(f"qro{i}", [128, 1024], BF16, 2, 2) for i in range(2)]
    e_s = [sb(f"e{i}", [128, 512], BF16, 3, 3) for i in range(6)]
    rr_s = [sb(f"rr{i}", [128, 512], BF16, 3, 3) for i in range(2)]
    qin_s = [sb(f"qin{i}", [128, 1024], BF16, 3, 3) for i in range(2)]
    rcl_s = sb("rcl_s", [1, 512], F32, 3, 3)
    rcb_s = sb("rcb_s", [1, 512], BF16, 3, 3)
    rcl2_s = sb("rcl2_s", [1, 512], F32, 3, 3)
    rcb2_s = sb("rcb2_s", [1, 512], BF16, 3, 3)
    xr_s = [sb(f"xr{i}", [128, 512], BF16, 4, 4) for i in range(3)]
    ob_s = [sb(f"ob{i}", [128, 512], BF16, 4, 4) for i in range(2)]
    sq_s = [sb(f"sq{i}", [128, 512], BF16, 4, 4) for i in range(2)]
    srow = sb("srow", [1, 1024], F32, 5, 5)
    sqrow = sb("sqrow", [1, 1024], F32, 5, 5)
    nm_r = sb("nm_r", [1, 1024], F32, 5, 5)
    ex2_r = sb("ex2_r", [1, 1024], F32, 5, 5)
    var_r = sbd("var_r", [1, 1024], F32, 5, 5)
    l1_r = sb("l1_r", [1, 1024], F32, 5, 5)
    rstd_r = sb("rstd_r", [1, 1024], F32, 5, 5)
    nmr_r = sb("nmr_r", [1, 1024], F32, 5, 5)
    rstdb_r = sb("rstdb_r", [1, 1024], BF16, 5, 5)
    nmrb_r = sb("nmrb_r", [1, 1024], BF16, 5, 5)
    rep_a = sb("rep_a", [128, 1024], BF16, 5, 5)
    rep_b = sb("rep_b", [128, 1024], BF16, 5, 5)
    ntmp_s = [sb(f"ntmp{i}", [128, 1024], BF16, 5, 5) for i in range(2)]
    ei_r = sb("ei_r", [1, 1024], F32, 5, 5)
    vu_r = sb("vu_r", [1, 1024], F32, 5, 5)
    pr_r = sb("pr_r", [1, 1], F32, 5, 5)
    ccs_s = sb("ccs_s", [2, 64], F32, 5, 5)
    wqp_s = sb("wqp_s", [128, 2048], BF16, 4, 6)
    wr_s = sb("wr_s", [64, 48], BF16, 4, 6)
    akt_s = sb("akt_s", [64, 1024], BF16, 4, 6)
    flr_sb = sb("flr_sb", [128, 1024], F32, 5, 6)
    qt_s = sbd("qt_s", [128, 1024], BF16, 6, 6)
    rq_s = sb("rq_s", [128, 1024], BF16, 6, 6)
    rk_s = sb("rk_s", [128, 1024], BF16, 6, 6)
    sc_f = [sb(f"scf{i}", [128, 1024], F32, 6, 6) for i in range(2)]
    s2_f = [sb(f"s2f{i}", [128, 1024], F32, 6, 6) for i in range(2)]
    m8_s = sb("m8_s", [128, 8], F32, 6, 6)
    mx_s = sb("mx_s", [128, 1], F32, 6, 6)
    nmx_s = sb("nmx_s", [128, 1], F32, 6, 6)
    e2_s = sb("e2_s", [128, 1024], BF16, 6, 6)
    mkb_s = sb("mkb_s", [128, 1024], BF16, 6, 6)
    emb_s = sbd("emb_s", [128, 8 * 1024], BF16, 6, 8)
    rw_s = sb("rw_s", [128, 8], F32, 6, 8)
    u1p_s = [sb(f"u1p{i}", [128, 4096], BF16, 5, 7) for i in range(2)]
    z1g_s = sb("z1g_s", [128, 8 * 1024], BF16, 6, 7)
    lrn_s = sb("lrn_s", [1, 1024], F32, 6, 8)
    sig_s = sbd("sig_s", [1, 1024], F32, 6, 8)
    g0_r = sb("g0_r", [1, 1024], F32, 6, 8)
    ge_r = sb("ge_r", [1, 1024], F32, 6, 8)
    gate_r = sbd("gate_r", [1, 1024], F32, 6, 8)
    gcol_s = sbd("gcol_s", [128, 8], F32, 8, 8)
    wtb_s = sb("wtb_s", [128, 8 * 1024], BF16, 8, 9)
    wf_s = sb("wf_s", [128, 8 * 1024], F32, 8, 8)
    avd_s = sb("avd_s", [128, 2 * 1024], BF16, 8, 9)
    wvu_s = sb("wvu_s", [128, 2 * 4096], BF16, 7, 9)
    yb_s = [sb(f"yb{i}", [128, 512], BF16, 9, 9) for i in range(2)]
    sq2_s = [sb(f"sq2{i}", [128, 512], BF16, 9, 9) for i in range(2)]
    sum2_r = sb("sum2_r", [1, 1024], F32, 10, 10)
    sq2r_r = sb("sq2r_r", [1, 1024], F32, 10, 10)
    nm2_r = sb("nm2_r", [1, 1024], F32, 10, 10)
    ex22_r = sb("ex22_r", [1, 1024], F32, 10, 10)
    var2_r = sb("var2_r", [1, 1024], F32, 10, 10)
    l2_r = sb("l2_r", [1, 1024], F32, 10, 10)
    rstd2_r = sb("rstd2_r", [1, 1024], F32, 10, 10)
    nmr2_r = sb("nmr2_r", [1, 1024], F32, 10, 10)
    rep2a = sb("rep2a", [128, 1024], F32, 10, 10)
    rep2b = sb("rep2b", [128, 1024], F32, 10, 10)
    outb_s = [sb(f"outb{i}", [128, 1024], F32, 10, 10) for i in range(4)]

    PA = nc.alloc_psum_tensor("PA", [128, 512], F32)
    PB = nc.alloc_psum_tensor("PB", [128, 512], F32)
    PC = nc.alloc_psum_tensor("PC", [128, 512], F32)
    PD = nc.alloc_psum_tensor("PD", [128, 512], F32)
    PE_ = nc.alloc_psum_tensor("PE", [128, 512], F32)
    PF = nc.alloc_psum_tensor("PF", [128, 512], F32)
    PG = nc.alloc_psum_tensor("PG", [128, 512], F32)
    PH = nc.alloc_psum_tensor("PH", [128, 512], F32)

    semnames = ["gp", "sgc", "gx", "gk0", "gk1", "gv0", "gv1", "gr0", "gr1", "gr2", "gf", "ssc", "sq0", "sq1", "so0", "so1", "sp", "sc", "sv", "scc"]
    sems = {}
    for s in semnames:
        sems[s] = nc.semaphore(s).__enter__()

    S = Sched(semnames)
    S.sems = sems

    def dma(engq, sem, out_ap, in_ap):
        S.wait(engq, sem, S.cnt[sem])
        def f(e, _o=out_ap, _i=in_ap, _s=sem):
            e.dma_start(out=_o, in_=_i).then_inc(sems[_s], 16)
        S.emit(engq, f)
        return S.bump(sem, 16)

    def mm(out_ap, lhsT, rhs, start, stop, inc=False):
        def f(e, _o=out_ap, _l=lhsT, _r=rhs, _st=start, _sp=stop, _i=inc):
            ins = e.matmul(_o, _l, _r, start=_st, stop=_sp, skip_group_check=True)
            if _i:
                ins.then_inc(sems["sp"], 1)
        S.emit("tensor", f)
        return S.bump("sp", 1) if inc else None

    def tr(out_ap, in_ap, ident_ap=None, inc=False):
        def f(e, _o=out_ap, _i=in_ap, _id=ident_ap, _n=inc):
            ins = e.transpose(_o, _i, _id if _id is not None else id_s[:])
            if _n:
                ins.then_inc(sems["sp"], 1)
        S.emit("tensor", f)
        return S.bump("sp", 1) if inc else None

    def act(out_ap, in_ap, func, bias=0.0, scale=1.0, accum=None, inc=True):
        def f(e, _o=out_ap, _i=in_ap, _f=func, _b=bias, _s=scale, _a=accum, _n=inc):
            ins = e.activation(_o, _i, _f, bias=_b, scale=_s, accum_out=_a)
            if _n:
                ins.then_inc(sems["sc"], 1)
        S.emit("scalar", f)
        return S.bump("sc", 1) if inc else None

    def vec(opname, *args, inc=True, **kw):
        def f(e, _op=opname, _a=args, _k=kw, _n=inc):
            ins = getattr(e, _op)(*_a, **_k)
            if _n:
                ins.then_inc(sems["sv"], 1)
        S.emit("vector", f)
        return S.bump("sv", 1) if inc else None

    c_ld = 0
    for dst, src in [
        (id_s[:], ident), (idf_s[:], identf), (onc_s[:], ones_c), (onr_s[:], ones_r),
        (onrf_s[:], ones_rf), (btbl_s[:], btbl),
        (on5_s[:], ones512), (bu1_s[:], fbu1), (cst_s[:], crow),
        (gw_s[:], fgw), (gb_s[:], fgb),
        (ck_lo[:], ck_t[:, 0:1024]), (sk_lo[:], sk_t[:, 0:1024]),
        (ck_hi[:], ck_t[:, 1024:2048]), (sk_hi[:], sk_t[:, 1024:2048]),
    ]:
        c_ld = dma("sync", "ssc", dst, src)
    dma("gpsimd", "sgc", wu2_s[:], fu2)
    dma("gpsimd", "sgc", bkb_s[:], bk_r)
    dma("gpsimd", "sgc", bvb_s[:], bv_r)
    g_cst = dma("gpsimd", "sgc", bu2b_s[:], fbu2)
    v_bvb = 0

    S.stop_mark(0)
    ev_xc = [[0, 0], [0, 0]]
    x_used_pe = 0
    kp_load = [0, 0]
    vp_load = [0, 0]
    kp_free_pe = [0, 0]
    vp_free_pe = [0, 0]
    ktmp_free_v = [0, 0]
    psAB_free_sc = [0, 0]

    S.wait("tensor", "ssc", c_ld)
    S.wait("scalar", "ssc", c_ld)
    for ch in (0, 1):
        if ch == 1:
            S.wait("gpsimd", "sp", x_used_pe)
        ev_xh = [0, 0]
        for hf in (0, 1):
            ev_xh[hf] = dma("gpsimd", "gx",
                            xcb[:, hf * 16384:(hf + 1) * 16384], xsw[ch, hf])
        ev_xc[ch] = ev_xh
        for g in range(KVH):
            buf = g % 2
            S.wait("gpsimd", "sp", kp_free_pe[buf])
            kp_load[buf] = dma("gpsimd", f"gk{buf}", kp_s[buf][:], wkT[g])
            for qtc in (0, 1):
                ps = PA if qtc == 0 else PB
                S.wait("tensor", f"gk{buf}", kp_load[buf])
                S.wait("tensor", "gx", ev_xc[ch][qtc])
                S.wait("tensor", "sc", psAB_free_sc[qtc])
                S.wait("tensor", "sgc", g_cst)
                for k in range(32):
                    mm(ps[:], kp_s[buf][:, k * 128:(k + 1) * 128],
                       xcb[:, qtc * 16384 + k * 512: qtc * 16384 + k * 512 + 512],
                       start=(k == 0), stop=False)
                mm(ps[:], bkb_s[0:1, g * 128:(g + 1) * 128], on5_s[0:1, :],
                   start=False, stop=True, inc=True)
                x_used_pe = kp_free_pe[buf] = S.cnt["sp"]
                tb = qtc
                S.wait("scalar", "sp", S.cnt["sp"])
                S.wait("scalar", "sv", ktmp_free_v[tb])
                psAB_free_sc[qtc] = act(ktmp_s[tb][:], ps[:], AF.Copy)
                cs = ch * 1024 + qtc * 512
                ctab = ck_lo if ch == 0 else ck_hi
                stab = sk_lo if ch == 0 else sk_hi
                co = qtc * 512
                dst = kbuf[:, g * 2048 + cs: g * 2048 + cs + 512]
                S.wait("vector", "sc", S.cnt["sc"])
                S.wait("vector", "ssc", c_ld)
                vec("tensor_copy", kswp_s[tb][0:64, :], ktmp_s[tb][64:128, :])
                vec("tensor_copy", kswp_s[tb][64:128, :], ktmp_s[tb][0:64, :])
                vec("tensor_tensor", dst, ktmp_s[tb][:], ctab[:, co:co + 512], ALU.mult)
                vec("tensor_tensor", kswp_s[tb][:], kswp_s[tb][:], stab[:, co:co + 512],
                    ALU.mult)
                ktmp_free_v[tb] = vec("tensor_tensor", dst, dst, kswp_s[tb][:], ALU.add)
        for vp in range(4):
            buf = vp % 2
            S.wait("gpsimd", "sp", vp_free_pe[buf])
            vp_load[buf] = dma("gpsimd", f"gv{buf}", vp_s[buf][:], wvT[vp])
            for sl in range(8):
                slot = ch * 8 + sl
                ps = PA if sl % 2 == 0 else PB
                S.wait("tensor", f"gv{buf}", vp_load[buf])
                S.wait("tensor", "gx", ev_xc[ch][sl // 4])
                S.wait("tensor", "sgc", g_cst)
                S.wait("tensor", "sc", psAB_free_sc[sl % 2])
                for k in range(32):
                    mm(ps[:, 0:256],
                       xcb[:, (sl // 4) * 16384 + k * 512 + (sl % 4) * 128:
                           (sl // 4) * 16384 + k * 512 + (sl % 4) * 128 + 128],
                       vp_s[buf][:, k * 256:(k + 1) * 256],
                       start=(k == 0), stop=False)
                mm(ps[:, 0:256], onr_s[0:1, :], bvb_s[0:1, vp * 256:(vp + 1) * 256],
                   start=False, stop=True, inc=True)
                x_used_pe = vp_free_pe[buf] = S.cnt["sp"]
                S.wait("scalar", "sp", S.cnt["sp"])
                psAB_free_sc[sl % 2] = act(
                    vbuf[:, slot * 1024 + vp * 256: slot * 1024 + vp * 256 + 256],
                    ps[:, 0:256], AF.Copy)

    if dbg:
        S.wait("sync", "sv", S.cnt["sv"])
        S.wait("sync", "sc", S.cnt["sc"])
        dma("sync", "so0", dbg_t["d_k"], kbuf[:])
        dma("sync", "so0", dbg_t["d_v"], vbuf[:])

    S.stop_mark(1)
    ph1_pe = S.cnt["sp"]
    S.wait("gpsimd", "sp", ph1_pe)
    g_cst = dma("gpsimd", "sgc", bqb_s[:], bq_r)
    S.wait("sync", "sp", ph1_pe)
    c_ld = dma("sync", "ssc", tri_s[:], tri_m)

    qp_load = [0, 0]
    qp_free_pe = [0, 0]
    qro_free_st = [0, 0]
    qro_v = [0, 0]
    qst_ev = {}
    for h in range(H):
        buf = h % 2
        S.wait("gpsimd", "sp", qp_free_pe[buf])
        qp_load[buf] = dma("gpsimd", f"gk{buf}", kp_s[buf][:], wqT[h])
        for qtc in (0, 1):
            ps = PA if qtc == 0 else PB
            S.wait("tensor", f"gk{buf}", qp_load[buf])
            S.wait("tensor", "gx", ev_xc[1][qtc])
            S.wait("tensor", "sgc", g_cst)
            S.wait("tensor", "sc", psAB_free_sc[qtc])
            for k in range(32):
                mm(ps[:], kp_s[buf][:, k * 128:(k + 1) * 128],
                   xcb[:, qtc * 16384 + k * 512: qtc * 16384 + k * 512 + 512],
                   start=(k == 0), stop=False)
            mm(ps[:], bqb_s[0:1, h * 128:(h + 1) * 128], on5_s[0:1, :],
               start=False, stop=True, inc=True)
            qp_free_pe[buf] = S.cnt["sp"]
            tb = qtc
            S.wait("scalar", "sp", S.cnt["sp"])
            S.wait("scalar", "sv", ktmp_free_v[tb])
            psAB_free_sc[qtc] = act(ktmp_s[tb][:], ps[:], AF.Copy)
            co = qtc * 512
            dst = qro_s[buf][:, qtc * 512: qtc * 512 + 512]
            S.wait("vector", "sc", S.cnt["sc"])
            S.wait("vector", f"so{buf}", qro_free_st[buf])
            vec("tensor_copy", kswp_s[tb][0:64, :], ktmp_s[tb][64:128, :])
            vec("tensor_copy", kswp_s[tb][64:128, :], ktmp_s[tb][0:64, :])
            vec("tensor_tensor", dst, ktmp_s[tb][:], ck_hi[:, co:co + 512], ALU.mult)
            vec("tensor_tensor", kswp_s[tb][:], kswp_s[tb][:], sk_hi[:, co:co + 512],
                ALU.mult)
            ktmp_free_v[tb] = vec("tensor_tensor", dst, dst, kswp_s[tb][:], ALU.add)
            qro_v[buf] = ktmp_free_v[tb]
        S.wait("sync", "sv", qro_v[buf])
        qro_free_st[buf] = dma("sync", f"so{buf}", qrope_d[h * 128:(h + 1) * 128, :],
                               qro_s[buf][:])
        qst_ev[h] = qro_free_st[buf]
    ev_q_done_st = [S.cnt["so0"], S.cnt["so1"]]

    S.stop_mark(2)
    GRAN = {0: [(sg_, sg_ + 4) for sg_ in range(8)],
            1: [(sg_, sg_) for sg_ in range(16)]}
    qin_ld = [0, 0]
    qin_free_pe = [0, 0]
    pc_free_v = [0, 0]
    pd_free_sc = [0, 0]
    pe_free_sc = 0
    rr_free_v = [0, 0]
    exp_on_ps = [0, 0, 0]
    rcl2 = [rcl_s, rcl2_s]
    rcb2 = [rcb_s, rcb2_s]
    pend_tail = []
    q_evac_done = max(psAB_free_sc)
    S.wait("tensor", "ssc", c_ld)
    S.wait("gpsimd", "sp", S.cnt["sp"])
    g_vd = dma("gpsimd", "gf", wvd_s[:], fvd)
    avp_load = [0, 0]
    avp_load[0] = dma("gpsimd", "gk0", avpan_s[0][:], favT[0])
    avp_load[1] = dma("gpsimd", "gk1", avpan_s[1][:], favT[1])

    def flush_tail():
        while pend_tail:
            pend_tail.pop(0)()

    for h in range(H):
        buf = h % 2
        g = h // GROUP
        if h == 0:
            S.wait("tensor", "sc", q_evac_done)
        S.wait("sync", f"so{buf}", qst_ev[h])
        S.wait("sync", "sp", qin_free_pe[buf])
        qin_ld[buf] = dma("sync", f"sq{buf}", qin_s[buf][:],
                          qrope_d[h * 128:(h + 1) * 128, :])
        for c in (0, 1):
            grans = GRAN[c]
            nsl = len(grans)
            pav = PC if c == 0 else PF
            pde = PD if c == 0 else PG
            sc_grp = {}

            def emit_scores(gi, slot):
                ps = (PA, PB, PH)[gi % 3]
                S.wait("tensor", f"sq{buf}", qin_ld[buf])
                S.wait("tensor", "sc", exp_on_ps[gi % 3])
                diag = gi >= nsl - 4
                mm(ps[:], kbuf[:, g * 2048 + slot * 128: g * 2048 + slot * 128 + 128],
                   qin_s[buf][:, c * 512:(c + 1) * 512],
                   start=True, stop=not diag, inc=not diag)
                if diag:
                    j = gi - (nsl - 4)
                    mm(ps[:], id_s[:], tri_s[:, j * 512:(j + 1) * 512],
                       start=False, stop=True, inc=True)
                sc_grp[gi] = S.cnt["sp"]

            emit_scores(*grans[0])
            flush_tail()
            if nsl > 1:
                emit_scores(*grans[1])
            exp_ev = {}
            pend_den = []
            for gi, slot in grans:
                ps = (PA, PB, PH)[gi % 3]
                eb = e_s[gi % 6]
                gidx = gi if c == 0 else 8 + gi
                S.wait("scalar", "sp", sc_grp[gi])
                exp_ev[gi] = exp_on_ps[gi % 3] = act(
                    eb[:], ps[:], AF.Exp,
                    bias=btbl_s[:, gidx:gidx + 1], scale=SC_ATT)
                if gi + 2 < nsl:
                    emit_scores(*grans[gi + 2])
                S.wait("tensor", "sc", exp_ev[gi])
                if gi == 0:
                    S.wait("tensor", "sc", pd_free_sc[c])
                    S.wait("tensor", "sv", pc_free_v[c])
                mm(pav[:], vbuf[:, slot * 1024 + g * 128: slot * 1024 + g * 128 + 128],
                   eb[:], start=(gi == 0), stop=(gi == nsl - 1), inc=(gi == nsl - 1))
                pend_den.append(gi)
                if len(pend_den) == 4 or gi == nsl - 1:
                    for dj in pend_den:
                        mm(pde[0:1, :], onc_s[:], e_s[dj % 6][:],
                           start=(dj == 0), stop=(dj == nsl - 1),
                           inc=(dj == nsl - 1))
                    pend_den = []
            av_grp = S.cnt["sp"]

            def emit_tail(h=h, c=c, av_grp=av_grp, pav=pav, pde=pde):
                nonlocal pe_free_sc
                S.wait("scalar", "sp", av_grp)
                act(rcl2[c][:], pde[0:1, :], AF.Ln)
                pd_free_sc[c] = act(rcb2[c][:], rcl2[c][:], AF.Exp, scale=-1.0)
                S.wait("tensor", "sc", max(pd_free_sc[c], pe_free_sc))
                mm(PE_[:], onr_s[0:1, :], rcb2[c][0:1, :], start=True, stop=True,
                   inc=True)
                rep_grp = S.cnt["sp"]
                S.wait("scalar", "sp", rep_grp)
                S.wait("scalar", "sv", rr_free_v[c])
                pe_free_sc = act(rr_s[c][:], PE_[:], AF.Copy)
                S.wait("vector", "sc", pe_free_sc)
                S.wait("vector", "sp", av_grp)
                pc_free_v[c] = vec("tensor_tensor",
                                   abuf[:, h * 1024 + c * 512: h * 1024 + c * 512 + 512],
                                   pav[:], rr_s[c][:], ALU.mult)
                rr_free_v[c] = pc_free_v[c]

            pend_tail.append(emit_tail)
        qin_free_pe[buf] = S.cnt["sp"]
    flush_tail()
    attn_done_v = S.cnt["sv"]

    if dbg:
        S.wait("sync", "sv", attn_done_v)
        dma("sync", "so0", dbg_t["d_at"], abuf[:])

    S.stop_mark(3)
    g_bob = dma("gpsimd", "gf", bob_s[:], bo_r)
    pend_stats = []
    wo_load = [0, 0]
    wo_free_pe = [0, 0]
    xr_ld = [0, 0, 0]
    xr_free_v = [0, 0, 0]
    ob_free_v = [0, 0]
    sq_free_pe = [0, 0]
    sq_free_sc = [0, 0]
    gcnt = 0
    for p in range(32):
        buf = p % 2
        S.wait("gpsimd", "sp", wo_free_pe[buf])
        if p == 1:
            S.wait("gpsimd", "sv", attn_done_v)
        wo_load[buf] = dma("gpsimd", f"gv{buf}", wop_s[buf][:], woT[p])
        if True:
            dtile = p
            for qtc in (0, 1):
                ps = PA if gcnt % 2 == 0 else PB
                xb = gcnt % 3
                if gcnt == 0:
                    S.wait("gpsimd", "sv", attn_done_v)
                S.wait("gpsimd", "sv", xr_free_v[xb])
                xr_ld[xb] = dma("gpsimd", f"gr{xb}", xr_s[xb][:],
                                xT[dtile * 128:(dtile + 1) * 128,
                                   1024 + qtc * 512: 1024 + qtc * 512 + 512])
                if len(pend_stats) > 1:
                    pend_stats.pop(0)()
                S.wait("tensor", f"gv{buf}", wo_load[buf])
                S.wait("tensor", "gf", g_bob)
                if gcnt == 0:
                    S.wait("tensor", "sv", attn_done_v)
                S.wait("tensor", "sc", psAB_free_sc[gcnt % 2])
                for k in range(32):
                    mm(ps[:], wop_s[buf][:, k * 128:(k + 1) * 128],
                       abuf[:, k * 1024 + qtc * 512: k * 1024 + qtc * 512 + 512],
                       start=(k == 0), stop=False)
                mm(ps[:], bob_s[0:1, dtile * 128:(dtile + 1) * 128], on5_s[0:1, :],
                   start=False, stop=True, inc=True)
                wo_free_pe[buf] = S.cnt["sp"]
                S.wait("scalar", "sp", S.cnt["sp"])
                S.wait("scalar", "sv", ob_free_v[gcnt % 2])
                psAB_free_sc[gcnt % 2] = act(ob_s[gcnt % 2][:], ps[:], AF.Copy)
                x1sl = x12[:, dtile * 1024 + qtc * 512: dtile * 1024 + qtc * 512 + 512]
                S.wait("vector", "sc", S.cnt["sc"])
                S.wait("vector", f"gr{xb}", xr_ld[xb])
                ob_free_v[gcnt % 2] = vec("tensor_tensor", x1sl, ob_s[gcnt % 2][:],
                                          xr_s[xb][:], ALU.add)
                xr_free_v[xb] = ob_free_v[gcnt % 2]
                S.wait("scalar", "sv", S.cnt["sv"])
                S.wait("scalar", "sp", sq_free_pe[gcnt % 2])
                sq_free_sc[gcnt % 2] = act(sq_s[gcnt % 2][:], x1sl, AF.Square)
                sc_now, sv_now = S.cnt["sc"], S.cnt["sv"]

                def emit_stats(x1sl=x1sl, sqb=sq_s[gcnt % 2], qtc=qtc, gcnt=gcnt,
                               sc_now=sc_now, sv_now=sv_now,
                               first=(p == 0),
                               last=(p == 31)):
                    S.wait("tensor", "sc", sc_now)
                    S.wait("tensor", "sv", sv_now)
                    pss = PC if qtc == 0 else PD
                    psq = PE_ if qtc == 0 else PF
                    mm(pss[0:1, :], onc_s[:], x1sl, start=first, stop=last)
                    mm(psq[0:1, :], onc_s[:], sqb[:], start=first, stop=last,
                       inc=True)
                    sq_free_pe[gcnt % 2] = S.cnt["sp"]

                pend_stats.append(emit_stats)
                gcnt += 1
    while pend_stats:
        pend_stats.pop(0)()
    stats_pe = S.cnt["sp"]

    S.stop_mark(4)
    u1_load = [0, 0]
    S.wait("gpsimd", "sp", stats_pe)
    u1_load[0] = dma("gpsimd", "gv0", u1p_s[0][:], fu1[0])
    u1_load[1] = dma("gpsimd", "gv1", u1p_s[1][:], fu1[1])
    avp_sc = 0
    avp_free_pe = [0, 0]
    for st in range(8):
        buf = st % 2
        S.wait("tensor", f"gk{buf}", avp_load[buf])
        S.wait("tensor", "gf", g_vd)
        S.wait("tensor", "sc", avp_sc)
        for k in range(32):
            mm(PG[:, 0:256], avpan_s[buf][:, k * 128:(k + 1) * 128],
               wvd_s[:, k * 256:(k + 1) * 256], start=(k == 0), stop=(k == 31),
               inc=(k == 31))
        avp_free_pe[buf] = S.cnt["sp"]
        if st + 2 < 8:
            S.wait("gpsimd", "sp", avp_free_pe[buf])
            avp_load[buf] = dma("gpsimd", f"gk{buf}", avpan_s[buf][:], favT[st + 2])
        S.wait("scalar", "sp", S.cnt["sp"])
        avp_sc = act(avp_s[:, st * 256:(st + 1) * 256], PG[:, 0:256], AF.Copy)
    avp_done = S.cnt["sc"]

    S.wait("vector", "sp", stats_pe)
    vec("tensor_copy", srow[0:1, 0:512], PC[0:1, :])
    vec("tensor_copy", srow[0:1, 512:1024], PD[0:1, :])
    vec("tensor_copy", sqrow[0:1, 0:512], PE_[0:1, :])
    v_stats = vec("tensor_copy", sqrow[0:1, 512:1024], PF[0:1, :])
    vec("tensor_scalar", nm_r[:], srow[:], -1.0 / D, None, ALU.mult)
    vec("tensor_scalar", ex2_r[:], sqrow[:], 1.0 / D, None, ALU.mult)
    vec("tensor_tensor", srow[:], nm_r[:], nm_r[:], ALU.mult)
    v_var = vec("tensor_tensor", var_r[:], ex2_r[:], srow[:], ALU.subtract)
    S.wait("scalar", "sv", v_var)
    act(l1_r[:], var_r[:], AF.Ln, bias=cst_s[0:1, 0:1])
    sc_rstd = act(rstd_r[:], l1_r[:], AF.Exp, scale=-0.5)
    sc_ei = act(ei_r[:], l1_r[:], AF.Exp, scale=-1.0)
    S.wait("vector", "sc", sc_rstd)
    vec("tensor_tensor", nmr_r[:], nm_r[:], rstd_r[:], ALU.mult)
    vec("tensor_copy", rstdb_r[:], rstd_r[:])
    v_rows = vec("tensor_copy", nmrb_r[:], nmr_r[:])
    S.wait("vector", "sc", sc_ei)
    v_vu = vec("tensor_tensor", vu_r[:], var_r[:], ei_r[:], ALU.mult)
    S.wait("scalar", "sv", v_vu)
    sc_lv = act(lv_s[:], vu_r[:], AF.Ln, bias=cst_s[0:1, 1:2])
    S.wait("vector", "sc", sc_lv)
    vec("memset", ccs_s[:], 0.0)
    vec("reduce_sum", pr_r[:], lv_s[:], AX)
    v_cc = vec("tensor_copy", ccs_s[0:1, 0:1], pr_r[:])
    g_wq = dma("gpsimd", "gf", wqp_s[:], fq)
    g_wr = dma("gpsimd", "gf", wr_s[:], fr)
    g_ak = dma("gpsimd", "gf", akt_s[:], fak)
    S.wait("gpsimd", "sv", v_cc)
    g_ccin = dma("gpsimd", "gf", cc_in, ccs_s[:])
    S.wait("gpsimd", "gf", g_ccin)

    def f_cc(e):
        e.collective_compute("AllReduce", ALU.add, replica_groups=[list(range(8))],
                             ins=[cc_in.opt()], outs=[cc_out.opt()]).then_inc(
            sems["scc"], 1)
    S.emit("gpsimd", f_cc)
    S.bump("scc", 1)
    S.wait("gpsimd", "scc", 1)
    g_ccout = dma("gpsimd", "gf", gsum_s[:], cc_out)

    S.wait("tensor", "sv", v_rows)
    mm(PA[:], onr_s[0:1, :], rstdb_r[0:1, 0:512], start=True, stop=True)
    mm(PB[:], onr_s[0:1, :], rstdb_r[0:1, 512:1024], start=True, stop=True)
    mm(PC[:], onr_s[0:1, :], nmrb_r[0:1, 0:512], start=True, stop=True)
    rep_pe = mm(PD[:], onr_s[0:1, :], nmrb_r[0:1, 512:1024], start=True, stop=True,
                inc=True)
    S.wait("scalar", "sp", rep_pe)
    act(rep_a[:, 0:512], PA[:], AF.Copy)
    act(rep_a[:, 512:1024], PB[:], AF.Copy)
    act(rep_b[:, 0:512], PC[:], AF.Copy)
    rep_sc = act(rep_b[:, 512:1024], PD[:], AF.Copy)
    psAB_free_sc = [rep_sc, rep_sc]
    S.wait("vector", "sc", rep_sc)
    ntmp_free = [0, 0]
    x2_half_v = [0, 0]
    for qtc in (0, 1):
        for dt_i in range(32):
            tb = dt_i % 2
            sl = slice(dt_i * 1024 + qtc * 512, dt_i * 1024 + qtc * 512 + 512)
            S.wait("vector", "sv", ntmp_free[tb])
            vec("tensor_tensor", ntmp_s[tb][:, 0:512], x12[:, sl],
                rep_a[:, qtc * 512:(qtc + 1) * 512], ALU.mult)
            ntmp_free[tb] = vec("tensor_tensor", x12[:, sl], ntmp_s[tb][:, 0:512],
                                rep_b[:, qtc * 512:(qtc + 1) * 512], ALU.add)
        x2_half_v[qtc] = S.cnt["sv"]
    x2_done_v = S.cnt["sv"]

    if dbg:
        S.wait("sync", "sv", x2_done_v)
        dma("sync", "so0", dbg_t["d_x2"], x12[:])
        S.wait("sync", "sc", sc_lv)
        dma("sync", "so0", dbg_t["d_rows"][0:1, :], lv_s[:])
        dma("sync", "so0", dbg_t["d_rows"][1:2, :], var_r[:])

    S.stop_mark(5)
    S.wait("sync", "sp", stats_pe)
    l_lr = dma("sync", "ssc", flr_sb[:], flr)
    for qtc in (0, 1):
        ps = PA if qtc == 0 else PB
        S.wait("tensor", "gf", g_wq)
        S.wait("tensor", "sv", x2_half_v[qtc])
        S.wait("tensor", "sc", psAB_free_sc[qtc])
        S.wait("tensor", "sc", max(exp_on_ps))
        for k in range(32):
            mm(ps[0:64, :], wqp_s[:, k * 64:(k + 1) * 64],
               x12[:, k * 1024 + qtc * 512: k * 1024 + qtc * 512 + 512],
               start=(k == 0), stop=(k == 31), inc=(k == 31))
        S.wait("scalar", "sp", S.cnt["sp"])
        psAB_free_sc[qtc] = act(qt_s[0:64, qtc * 512:(qtc + 1) * 512], ps[0:64, :],
                                AF.Copy)
    qt_sc = S.cnt["sc"]
    for dst, rhs, ld in [(rq_s, qt_s, qt_sc), (rk_s, akt_s, 0)]:
        for qtc in (0, 1):
            ps = PA if qtc == 0 else PB
            S.wait("tensor", "gf", max(g_wr, g_ak))
            S.wait("tensor", "sc", max(psAB_free_sc[qtc], ld))
            mm(ps[0:48, :], wr_s[0:64, :], rhs[0:64, qtc * 512:(qtc + 1) * 512],
               start=True, stop=True, inc=True)
            S.wait("scalar", "sp", S.cnt["sp"])
            psAB_free_sc[qtc] = act(dst[0:48, qtc * 512:(qtc + 1) * 512], ps[0:48, :],
                                    AF.Copy)
    rqrk_sc = S.cnt["sc"]
    tile_ps_free_v = 0
    rt_first = True

    def emit_rtile(tt):
        nonlocal tile_ps_free_v, rt_first
        fb = tt % 2
        S.wait("tensor", "sc", rqrk_sc)
        if rt_first:
            S.wait("tensor", "sc", pe_free_sc)
            S.wait("tensor", "sv", v_stats)
            rt_first = False
        S.wait("tensor", "sv", tile_ps_free_v)
        mm(PC[:], rq_s[0:48, tt * 128:(tt + 1) * 128], rk_s[0:48, 0:512],
           start=True, stop=True)
        mm(PD[:], rq_s[0:48, tt * 128:(tt + 1) * 128], rk_s[0:48, 512:1024],
           start=True, stop=True)
        mm(PE_[:], qt_s[0:64, tt * 128:(tt + 1) * 128], akt_s[0:64, 0:512],
           start=True, stop=True)
        rt_pe = mm(PF[:], qt_s[0:64, tt * 128:(tt + 1) * 128], akt_s[0:64, 512:1024],
                   start=True, stop=True, inc=True)
        S.wait("vector", "sp", rt_pe)
        S.wait("vector", "ssc", l_lr)
        vec("scalar_tensor_tensor", sc_f[fb][:, 0:512], PC[:], SC_RT,
            flr_sb[:, 0:512], ALU.mult, ALU.add)
        vec("scalar_tensor_tensor", sc_f[fb][:, 512:1024], PD[:], SC_RT,
            flr_sb[:, 512:1024], ALU.mult, ALU.add)
        vec("scalar_tensor_tensor", s2_f[fb][:, 0:512], PE_[:], 0.125,
            flr_sb[:, 0:512], ALU.mult, ALU.add)
        ps_done = vec("scalar_tensor_tensor", s2_f[fb][:, 512:1024], PF[:], 0.125,
                      flr_sb[:, 512:1024], ALU.mult, ALU.add)
        tile_ps_free_v = ps_done
        vec("max", m8_s[:], sc_f[fb][:])
        vec("reduce_max", mx_s[:], s2_f[fb][:], AX)
        vec("tensor_scalar", nmx_s[:], mx_s[:], -1.0, None, ALU.mult)
        v_nmx = S.cnt["sv"]
        S.wait("scalar", "sv", v_nmx)
        sc_e2 = act(e2_s[:], s2_f[fb][:], AF.Exp, bias=nmx_s[:])
        vec("tensor_scalar", mkb_s[:], sc_f[fb][:], m8_s[:, 7:8], None, ALU.is_ge)
        S.wait("vector", "sc", sc_e2)
        vec("tensor_tensor", emb_s[:, tt * 1024:(tt + 1) * 1024], e2_s[:], mkb_s[:],
            ALU.mult)
        vec("reduce_sum", mx_s[:], emb_s[:, tt * 1024:(tt + 1) * 1024], AX)
        vec("reciprocal", rw_s[:, tt:tt + 1], mx_s[:])

    S.stop_mark(6)
    u1_free_pe = [0, 0]
    z1_sc = {}
    for m in range(8):
        buf = m % 2
        if m >= 2:
            S.wait("gpsimd", "sp", u1_free_pe[buf])
            u1_load[buf] = dma("gpsimd", f"gv{buf}", u1p_s[buf][:], fu1[m])
        for qtc in (0, 1):
            ps = PA if qtc == 0 else PB
            S.wait("tensor", f"gv{buf}", u1_load[buf])
            S.wait("tensor", "sv", x2_half_v[qtc])
            if m == 0:
                S.wait("tensor", "sc", max(avp_sc, rqrk_sc))
            S.wait("tensor", "sc", z1_sc.get((m - 1, qtc), 0))
            for k in range(32):
                mm(ps[:], u1p_s[buf][:, k * 128:(k + 1) * 128],
                   x12[:, k * 1024 + qtc * 512: k * 1024 + qtc * 512 + 512],
                   start=(k == 0), stop=(k == 31), inc=(k == 31))
            u1_free_pe[buf] = S.cnt["sp"]
            S.wait("scalar", "sp", S.cnt["sp"])
            z1_sc[(m, qtc)] = act(
                z1g_s[:, m * 1024 + qtc * 512: m * 1024 + qtc * 512 + 512],
                ps[:], AF.Gelu_apprx_tanh, bias=bu1_s[:, m:m + 1])
        emit_rtile(m)
        S.wait("tensor", "sc", z1_sc[(m, 1)])
        if m == 0:
            S.wait("tensor", "sc", avp_done)
        for qtc in (0, 1):
            po = PG[0:1, :] if qtc == 0 else PG[64:65, :]
            mm(po, wu2_s[:, m:m + 1],
               z1g_s[:, m * 1024 + qtc * 512: m * 1024 + qtc * 512 + 512],
               start=(m == 0), stop=False)
            if m == 7:
                mm(po, bu2b_s[0:1, 0:1], on5_s[0:1, :], start=False, stop=True,
                   inc=(qtc == 1))
    lrn_pe = S.cnt["sp"]
    S.wait("scalar", "sp", lrn_pe)
    act(lrn_s[0:1, 0:512], PG[0:1, :], AF.Copy)
    lrn_sc = act(lrn_s[0:1, 512:1024], PG[64:65, :], AF.Copy)
    z1_done = S.cnt["sc"]
    em_done_v = S.cnt["sv"]

    if dbg:
        S.wait("sync", "sv", em_done_v)
        dma("sync", "so0", dbg_t["d_em"], emb_s[:])
        dma("sync", "so0", dbg_t["d_qt"][0:64, :], qt_s[0:64, :])
    S.stop_mark(6.25)
    act(sig_s[:], lrn_s[:], AF.Exp, scale=-1.0)
    act(sig_s[:], sig_s[:], AF.Ln, bias=cst_s[0:1, 1:2])
    sig_done = act(sig_s[:], sig_s[:], AF.Exp, scale=-1.0)

    S.wait("vector", "gf", g_ccout)
    vec("tensor_scalar", gsum_s[0:1, 0:1], gsum_s[0:1, 0:1], 1.0 / 8192.0,
        1e-6, ALU.mult, ALU.add)
    vec("reciprocal", gsum_s[0:1, 1:2], gsum_s[0:1, 0:1])
    S.wait("vector", "sc", sig_done)
    vec("tensor_scalar", g0_r[:], lv_s[:], gsum_s[0:1, 1:2], None, ALU.mult)
    vec("tensor_scalar", ge_r[:], sig_s[:], 2.5, None, ALU.mult)
    vec("scalar_tensor_tensor", g0_r[:], g0_r[:], 0.5, ge_r[:], ALU.mult, ALU.add)
    vec("tensor_scalar", g0_r[:], g0_r[:], 0.0, 5.0, ALU.max, ALU.min)
    vec("tensor_scalar", g0_r[:], g0_r[:], -0.5, 1.0 / 1.5, ALU.add, ALU.mult)
    vec("tensor_scalar", g0_r[:], g0_r[:], 0.0, 1.0, ALU.max, ALU.min)
    v_un = vec("tensor_scalar", g0_r[:], g0_r[:], gw_s[0:1, 0:1], gb_s[0:1, 0:1],
               ALU.mult, ALU.add)
    S.wait("scalar", "sv", v_un)
    act(gate_r[:], g0_r[:], AF.Exp, scale=-1.0)
    act(gate_r[:], gate_r[:], AF.Ln, bias=cst_s[0:1, 1:2])
    sc_g = act(gate_r[:], gate_r[:], AF.Exp, scale=-1.0)
    S.wait("vector", "sc", sc_g)
    vec("tensor_scalar", ge_r[:], gate_r[:], 0.05, None, ALU.is_ge)
    v_gate = vec("tensor_tensor", gate_r[:], gate_r[:], ge_r[:], ALU.mult)
    S.tail_v_gate = v_gate
    S.stop_mark(6.5)

    if dbg:
        S.wait("sync", "sc", avp_done)
        dma("sync", "so0", dbg_t["d_avp"], avp_s[:])
        S.wait("sync", "sc", sig_done)
        dma("sync", "so0", dbg_t["d_rows"][2:3, :], sig_s[:])

    S.stop_mark(7)
    v_gate = S.tail_v_gate
    S.wait("tensor", "sv", v_gate)
    for tt in range(8):
        mm(PE_[:, tt:tt + 1], gate_r[0:1, tt * 128:(tt + 1) * 128], onrf_s[0:1, 0:1],
           start=True, stop=True, inc=(tt == 7))
    S.wait("scalar", "sp", S.cnt["sp"])
    gcol_sc = act(gcol_s[:], PE_[:, 0:8], AF.Copy)
    S.wait("vector", "sc", gcol_sc)
    S.wait("vector", "sv", em_done_v)
    w_v = 0
    for tt in range(8):
        w_v = vec("tensor_scalar", wf_s[:, tt * 1024:(tt + 1) * 1024],
                  emb_s[:, tt * 1024:(tt + 1) * 1024], rw_s[:, tt:tt + 1],
                  gcol_s[:, tt:tt + 1], ALU.mult, ALU.mult)
    S.wait("tensor", "sv", w_v)
    tr_sc = 0
    for st in range(8):
        for half in (0, 1):
            S.wait("tensor", "sc", tr_sc)
            for q4 in range(4):
                tt = half * 4 + q4
                tr(PH[:, q4 * 128:(q4 + 1) * 128],
                   wf_s[:, tt * 1024 + st * 128: tt * 1024 + st * 128 + 128],
                   ident_ap=idf_s[:], inc=(q4 == 3))
            S.wait("scalar", "sp", S.cnt["sp"])
            tr_sc = act(wtb_s[:, st * 1024 + half * 512: st * 1024 + half * 512 + 512],
                        PH[:], AF.Copy)
    avd_sc = {}
    for ct in (0, 1):
        for qtc in (0, 1):
            ps = PA if qtc == 0 else PB
            S.wait("tensor", "sc", max(avp_done, tr_sc))
            for k in range(8):
                mm(ps[:], avp_s[:, k * 256 + ct * 128: k * 256 + ct * 128 + 128],
                   wtb_s[:, k * 1024 + qtc * 512: k * 1024 + qtc * 512 + 512],
                   start=(k == 0), stop=(k == 7), inc=(k == 7))
            S.wait("scalar", "sp", S.cnt["sp"])
            avd_sc[(ct, qtc)] = act(
                avd_s[:, ct * 1024 + qtc * 512: ct * 1024 + qtc * 512 + 512],
                ps[:], AF.Copy)
    avd_done = S.cnt["sc"]

    if dbg:
        S.wait("sync", "sc", gcol_sc)
        dma("sync", "so0", dbg_t["d_gc"], gcol_s[:])
        S.wait("sync", "sv", S.cnt["sv"])
        dma("sync", "so0", dbg_t["d_rows"][3:4, :], gate_r[:])

    S.stop_mark(8)
    S.wait("gpsimd", "sv", em_done_v)
    g_vu = dma("gpsimd", "gf", wvu_s[:], fvu)
    yb_free_v = [0, 0]
    sq2_sc = [0, 0]
    y_sc = [0, 0, 0, 0]
    gcnt = 0
    for dt_i in range(32):
        for qtc in (0, 1):
            ps = (PA, PB, PG, PH)[gcnt % 4]
            if len(pend_stats) > 1:
                pend_stats.pop(0)()
            S.wait("tensor", "gf", g_vu)
            S.wait("tensor", "sc", max(avd_done, y_sc[gcnt % 4]))
            for k in range(2):
                mm(ps[:], wvu_s[:, k * 4096 + dt_i * 128: k * 4096 + dt_i * 128 + 128],
                   avd_s[:, k * 1024 + qtc * 512: k * 1024 + qtc * 512 + 512],
                   start=(k == 0), stop=(k == 1), inc=(k == 1))
            S.wait("scalar", "sp", S.cnt["sp"])
            S.wait("scalar", "sv", yb_free_v[gcnt % 2])
            y_sc[gcnt % 4] = act(yb_s[gcnt % 2][:], ps[:], AF.Copy)
            x3sl = x3b[:, dt_i * 1024 + qtc * 512: dt_i * 1024 + qtc * 512 + 512]
            S.wait("vector", "sc", S.cnt["sc"])
            yb_free_v[gcnt % 2] = vec("tensor_tensor", x3sl, yb_s[gcnt % 2][:],
                                      x12[:, dt_i * 1024 + qtc * 512:
                                          dt_i * 1024 + qtc * 512 + 512], ALU.add)
            S.wait("scalar", "sv", S.cnt["sv"])
            S.wait("scalar", "sp", sq_free_pe[gcnt % 2])
            sq2_sc[gcnt % 2] = act(sq2_s[gcnt % 2][:], x3sl, AF.Square)
            sc_now, sv_now = S.cnt["sc"], S.cnt["sv"]

            def emit_stats2(x3sl=x3sl, sqb=sq2_s[gcnt % 2], qtc=qtc, gcnt=gcnt,
                            sc_now=sc_now, sv_now=sv_now,
                            first=(dt_i == 0), last=(dt_i == 31)):
                S.wait("tensor", "sc", sc_now)
                S.wait("tensor", "sv", sv_now)
                pss = PC if qtc == 0 else PD
                psq = PE_ if qtc == 0 else PF
                mm(pss[0:1, :], onc_s[:], x3sl, start=first, stop=last)
                mm(psq[0:1, :], onc_s[:], sqb[:], start=first, stop=last, inc=True)
                sq_free_pe[gcnt % 2] = S.cnt["sp"]

            pend_stats.append(emit_stats2)
            gcnt += 1
    while pend_stats:
        pend_stats.pop(0)()
    stats2_pe = S.cnt["sp"]

    if dbg:
        S.wait("sync", "sp", stats2_pe)
        dma("sync", "so0", dbg_t["d_x3"], x3b[:])

    S.stop_mark(9)
    S.wait("vector", "sp", stats2_pe)
    vec("tensor_copy", sum2_r[0:1, 0:512], PC[0:1, :])
    vec("tensor_copy", sum2_r[0:1, 512:1024], PD[0:1, :])
    vec("tensor_copy", sq2r_r[0:1, 0:512], PE_[0:1, :])
    vec("tensor_copy", sq2r_r[0:1, 512:1024], PF[0:1, :])
    vec("tensor_scalar", nm2_r[:], sum2_r[:], -1.0 / D, None, ALU.mult)
    vec("tensor_scalar", ex22_r[:], sq2r_r[:], 1.0 / D, None, ALU.mult)
    vec("tensor_tensor", sum2_r[:], nm2_r[:], nm2_r[:], ALU.mult)
    v_var2 = vec("tensor_tensor", var2_r[:], ex22_r[:], sum2_r[:], ALU.subtract)
    S.wait("scalar", "sv", v_var2)
    act(l2_r[:], var2_r[:], AF.Ln, bias=cst_s[0:1, 0:1])
    sc_rstd2 = act(rstd2_r[:], l2_r[:], AF.Exp, scale=-0.5)
    S.wait("vector", "sc", sc_rstd2)
    v_rows2 = vec("tensor_tensor", nmr2_r[:], nm2_r[:], rstd2_r[:], ALU.mult)
    S.wait("tensor", "sv", v_rows2)
    mm(PA[:], onrf_s[0:1, :], rstd2_r[0:1, 0:512], start=True, stop=True)
    mm(PB[:], onrf_s[0:1, :], rstd2_r[0:1, 512:1024], start=True, stop=True)
    mm(PC[:], onrf_s[0:1, :], nmr2_r[0:1, 0:512], start=True, stop=True)
    rep2_pe = mm(PD[:], onrf_s[0:1, :], nmr2_r[0:1, 512:1024], start=True, stop=True,
                 inc=True)
    S.wait("scalar", "sp", rep2_pe)
    act(rep2a[:, 0:512], PA[:], AF.Copy)
    act(rep2a[:, 512:1024], PB[:], AF.Copy)
    act(rep2b[:, 0:512], PC[:], AF.Copy)
    rep2_sc = act(rep2b[:, 512:1024], PD[:], AF.Copy)
    S.wait("vector", "sc", rep2_sc)
    S.wait("gpsimd", "sc", rep2_sc)
    out_free_st = [0, 0, 0, 0]
    for dt_i in range(32):
        tb = dt_i % 4
        gp = (dt_i % 3 == 2)
        eng = "gpsimd" if gp else "vector"
        sem = "gp" if gp else "sv"

        def go(opname, *args, _e=eng, _s=sem, inc=True):
            def f(e, _op=opname, _a=args, _n=inc):
                ins = getattr(e, _op)(*_a)
                if _n:
                    ins.then_inc(sems[_s], 1)
            S.emit(_e, f)
            return S.bump(_s, 1) if inc else None

        S.wait(eng, f"so{tb % 2}", out_free_st[tb])
        go("tensor_tensor", outb_s[tb][:], x3b[:, dt_i * 1024:(dt_i + 1) * 1024],
           rep2a[:], ALU.mult)
        v_o = go("tensor_tensor", outb_s[tb][:], outb_s[tb][:], rep2b[:], ALU.add)
        S.wait("sync", sem, v_o)
        out_free_st[tb] = dma("sync", f"so{tb % 2}",
                              out_d[dt_i * 128:(dt_i + 1) * 128, :], outb_s[tb][:])

    import os
    _pad = int(os.environ.get("PE_PAD", "0"))
    for _ in range(_pad):
        S.emit("tensor", lambda e: e.nop())

    S.wait("sync", "so0", S.cnt["so0"])
    S.wait("sync", "so1", S.cnt["so1"])

    if record:
        return AL.reqs

    with nc.Block() as blk:
        @blk.tensor
        def _(e):
            for f in S.q["tensor"]:
                f(e)

        @blk.vector
        def _(e):
            for f in S.q["vector"]:
                f(e)

        @blk.scalar
        def _(e):
            for f in S.q["scalar"]:
                f(e)

        @blk.gpsimd
        def _(e):
            for f in S.q["gpsimd"]:
                f(e)

        @blk.sync
        def _(e):
            for f in S.q["sync"]:
                f(e)

    return nc




def _prep_inputs(inputs):
    x = np.ascontiguousarray(np.asarray(inputs["L0_attn_x"], np.float32))
    wqkv = np.asarray(inputs["L0_attn_w_qkv"], np.float32)
    bqkv = np.asarray(inputs["L0_attn_b_qkv"], np.float32)
    wo = np.asarray(inputs["L0_attn_w_o"], np.float32)
    bo = np.asarray(inputs["L0_attn_b_o"], np.float32)

    perm = _perm_qkv_rows()
    wp = wqkv[perm]
    bp = bqkv[perm]

    def _swz(panels):
        NP, Dd, C = panels.shape
        return np.ascontiguousarray(
            panels.reshape(NP, 32, 128, C).swapaxes(1, 2).reshape(NP, 128, 32 * C))

    wqT_h = _swz(wp[:4096].reshape(H, 128, D).transpose(0, 2, 1))
    wkT_h = _swz(wp[4096:5120].reshape(KVH, 128, D).transpose(0, 2, 1))
    wvT_h = _swz(wp[5120:6144].reshape(4, 256, D).transpose(0, 2, 1))
    woT_h = _swz(wo.T.reshape(D, 32, 128).transpose(1, 0, 2))
    bq_h = np.ascontiguousarray(bp[:4096].reshape(1, 4096))
    bk_h = np.ascontiguousarray(bp[4096:5120].reshape(1, 1024))
    bv_h = np.ascontiguousarray(bp[5120:6144].reshape(1, 1024))
    bo_h = np.ascontiguousarray(bo.reshape(1, 4096))

    tri_h = np.ascontiguousarray(_tri_masks().transpose(1, 0, 2).reshape(128, 2048))
    id_h = np.eye(128, dtype=bfdt)
    idf_h = np.eye(128, dtype=np.float32)
    ones_c_h = np.ones((128, 1), bfdt)
    ones_r_h = np.ones((1, 128), bfdt)
    ones512_h = np.ones((1, 512), bfdt)
    ones_rf_h = np.ones((1, 128), np.float32)

    fq_h = _swz(np.asarray(inputs["L0_ffn_Wq"], np.float32).T[None])[0]
    fr_h = np.ascontiguousarray(np.asarray(inputs["L0_ffn_Wr"], np.float32).T)
    fak_h = np.ascontiguousarray(np.asarray(inputs["L0_ffn_aux_keys"], np.float32).T)
    favT_h = _swz(np.asarray(inputs["L0_ffn_aux_values"], np.float32)
                  .reshape(8, 128, D).transpose(0, 2, 1))
    fvd_h = _swz(np.asarray(inputs["L0_ffn_Wvd"], np.float32).T[None])[0]
    fvu_t = np.asarray(inputs["L0_ffn_Wvu"], np.float32).T
    fvu_h = np.ascontiguousarray(
        fvu_t.reshape(2, 128, D).swapaxes(0, 1).reshape(128, 2 * D))
    fu1_h = _swz(np.asarray(inputs["L0_ffn_Wu1"], np.float32)
                 .reshape(8, 128, D).transpose(0, 2, 1))
    fbu1_h = np.ascontiguousarray(
        np.asarray(inputs["L0_ffn_bu1"], np.float32).reshape(8, 128).T)
    fu2_h = np.ascontiguousarray(
        np.asarray(inputs["L0_ffn_Wu2"], np.float32).reshape(8, 128).T)
    fbu2_h = np.asarray(inputs["L0_ffn_bu2"], np.float32).reshape(1, 1)
    fgw_h = np.asarray(inputs["L0_ffn_gate_w1"], np.float32).reshape(1, 1)
    fgb_h = np.asarray(inputs["L0_ffn_gate_bias"], np.float32).reshape(1, 1)
    flr_h = np.ascontiguousarray(np.broadcast_to(
        np.asarray(inputs["L0_ffn_log_rel"], np.float32)[None, :], (128, 1024)))

    common = dict(wkT=wkT_h, wvT=wvT_h, wqT=wqT_h, woT=woT_h, bq_r=bq_h, bk_r=bk_h,
                  bv_r=bv_h, bo_r=bo_h, tri_m=tri_h, ident=id_h, identf=idf_h, ones_c=ones_c_h,
                  ones_r=ones_r_h, ones512=ones512_h, ones_rf=ones_rf_h,
                  fq=fq_h, fr=fr_h, fak=fak_h,
                  favT=favT_h, fvd=fvd_h, fvu=fvu_h, fu1=fu1_h, fbu1=fbu1_h,
                  fu2=fu2_h, fbu2=fbu2_h, fgw=fgw_h, fgb=fgb_h, flr=flr_h,
                  crow=np.array([[EPS, 1.0, 0, 0, 0, 0, 0, 0]], np.float32))

    in_maps = []
    for core in range(8):
        b, half = core // 2, core % 2
        stb = _slot_to_block(half)
        xTc = np.zeros((D, 2048), np.float32)
        pos = np.zeros(2048, np.float64)
        for s, bl in enumerate(stb):
            if bl >= 0:
                xTc[:, s * 128:(s + 1) * 128] = x[b, bl * 128:(bl + 1) * 128, :].T
                pos[s * 128:(s + 1) * 128] = np.arange(bl * 128, (bl + 1) * 128)
        ck_h, sk_h = _rope_tables(pos)
        m = dict(common)
        m["xT"] = xTc
        m["xsw"] = np.ascontiguousarray(
            xTc.reshape(32, 128, 2, 2, 512).transpose(2, 3, 1, 0, 4)
            .reshape(2, 2, 128, 32 * 512))
        m["ck_t"] = ck_h
        m["sk_t"] = sk_h
        m["btbl"] = _bias_tbl(half)
        in_maps.append(m)
    return in_maps


def kernel(**inputs):
    key = ("dbg" if _DBG else "std", _STOP)
    if key not in _CACHE:
        _CACHE[key] = _build_program(dbg=_DBG)
    nc = _CACHE[key]
    in_maps = _prep_inputs(inputs)
    res = run_bass_kernel_spmd(nc, in_maps, core_ids=list(range(8)))
    out = np.zeros((4, 2048, D), np.float32)
    for core in range(8):
        o = res.results[core]["out"]
        b, half = core // 2, core % 2
        for j, bl in enumerate(_own_blocks(half)):
            out[b, bl * 128:(bl + 1) * 128, :] = o[:, j * 128:(j + 1) * 128].T
    if _DBG:
        kernel._dbg = [res.results[c] for c in range(8)]
    return out


# version 12 (final)
# speedup vs baseline: 1.0042x; 1.0042x over previous
import numpy as np
import ml_dtypes
import concourse.bass as bass
import concourse.mybir as mybir
from concourse.bass_utils import run_bass_kernel_spmd

F32 = mybir.dt.float32
BF16 = mybir.dt.bfloat16
AF = mybir.ActivationFunctionType
ALU = mybir.AluOpType
AX = mybir.AxisListType.X
bfdt = ml_dtypes.bfloat16

D = 4096
H = 32
KVH = 8
HD = 128
GROUP = 4
SHIFT = 20.0
NEGBIG = -1e30
MASKV = -1e38
EPS = 1e-5
SC_ATT = 1.0 / float(np.sqrt(128.0))
SC_RT = 1.0 / float(np.sqrt(48.0))

_DBG = False
_STOP = 99
_CACHE = {}



def _slot_to_block(half):
    if half == 0:
        return [4, 5, 6, 7, 8, 9, 10, 11, 0, 1, 2, 3, 12, 13, 14, 15]
    return [-1, -1, -1, -1, 0, 1, 2, 3, 4, 5, 6, 7, 8, 9, 10, 11]


def _own_blocks(half):
    return _slot_to_block(half)[8:16]


def _perm_qkv_rows():
    perm = []
    for h in range(H):
        b0 = h * HD
        perm += [b0 + 2 * i for i in range(64)] + [b0 + 2 * i + 1 for i in range(64)]
    for g in range(KVH):
        b0 = 4096 + g * HD
        perm += [b0 + 2 * i for i in range(64)] + [b0 + 2 * i + 1 for i in range(64)]
    perm += list(range(5120, 6144))
    return np.array(perm)


def _rope_tables(positions):
    inv = 1.0 / (10000.0 ** (np.arange(0, HD, 2, dtype=np.float64) / HD))
    ang = positions[None, :].astype(np.float64) * inv[:, None]
    C = np.concatenate([np.cos(ang), np.cos(ang)], 0)
    Sg = np.concatenate([-np.sin(ang), np.sin(ang)], 0)
    return C.astype(bfdt), Sg.astype(bfdt)


def _tri_masks():
    ms = np.zeros((4, 128, 512), np.float32)
    kk = np.arange(128)[:, None]
    qq = np.arange(128)[None, :]
    blk = np.where(kk > qq, MASKV, 0.0)
    for j in range(4):
        ms[j, :, : j * 128] = MASKV
        ms[j, :, j * 128:(j + 1) * 128] = blk
    return ms.astype(bfdt)


def _bias_tbl(half):
    t = np.full((128, 24), -SHIFT, np.float32)
    if half == 0:
        t[:, 0:4] = NEGBIG
    else:
        t[:, 8:12] = NEGBIG
    return t




class Alloc:

    BASE = 16512
    CAP = 212600

    def __init__(self, nc, placement=None):
        self.nc = nc
        self.placement = placement
        self.reqs = []

    def sb(self, name, shape, dtype, lo, hi):
        per = int(np.prod(shape[1:])) * mybir.dt.size(dtype)
        size = (per + 31) // 32 * 32
        self.reqs.append((name, size, lo, hi))
        if self.placement is None:
            return self.nc.alloc_sbuf_tensor_at(
                name, list(shape), dtype, offset=self.BASE)
        off = self.placement[name]
        assert off + size <= self.CAP, (name, off, size)
        return self.nc.alloc_sbuf_tensor_at(
            name, list(shape), dtype, offset=self.BASE + off)

    @staticmethod
    def _try_solve(reqs):
        placed = []
        out = {}
        for name, size, lo, hi in reqs:
            if name in Alloc.PIN:
                off = Alloc.PIN[name]
                placed.append((off, size, lo, hi))
                out[name] = off
                continue
            best = None
            for off in sorted({0} | {o + s for (o, s, l, h) in placed}):
                if off + size > Alloc.CAP:
                    continue
                if all(hi < l or h < lo or off + size <= o or o + s <= off
                       for (o, s, l, h) in placed):
                    best = off
                    break
            if best is None:
                return None
            placed.append((best, size, lo, hi))
            out[name] = best
        return out

    PIN = {"kbuf": 0, "vbuf": 32768, "x12": 0,
           "xcb": 65536, "abuf": 65536, "x3b": 65536}

    @staticmethod
    def solve(reqs, cap):
        orders = [
            sorted(reqs, key=lambda r: (-(r[3] - r[2]), -r[1])),
            sorted(reqs, key=lambda r: -r[1]),
            sorted(reqs, key=lambda r: (-r[1], -(r[3] - r[2]))),
            sorted(reqs, key=lambda r: (r[2], -r[1])),
        ]
        for o in orders:
            o = ([r for r in o if r[0] in Alloc.PIN]
                 + [r for r in o if r[0] not in Alloc.PIN])
            res = Alloc._try_solve(o)
            if res is not None:
                return res
        raise MemoryError("SBUF solve failed under all orderings")


class Sched:
    ENGS = ("tensor", "vector", "scalar", "gpsimd", "sync")

    def __init__(self, semnames):
        self.q = {e: [] for e in self.ENGS}
        self.cnt = {s: 0 for s in semnames}
        self.last_wait = {e: {s: 0 for s in semnames} for e in self.ENGS}
        self.sems = None

    stopped = False

    def stop_mark(self, n):
        if n >= _STOP:
            self.stopped = True

    def emit(self, eng, fn):
        if self.stopped:
            return
        self.q[eng].append(fn)

    def wait(self, eng, sem, val):
        if self.stopped:
            return
        if val <= 0 or val <= self.last_wait[eng][sem]:
            return
        self.last_wait[eng][sem] = val
        sel = self

        def f(e, _s=sem, _v=val):
            e.wait_ge(sel.sems[_s], _v)
        self.q[eng].append(f)

    def bump(self, sem, by=1):
        self.cnt[sem] += by
        return self.cnt[sem]


_PLACEMENT = {}


def _build_program(dbg=False):
    if _PLACEMENT.get("key") != dbg:
        _PLACEMENT.pop("map", None)
        rec = _build_program_inner(dbg, record=True)
        _PLACEMENT["map"] = Alloc.solve(rec, Alloc.CAP)
        _PLACEMENT["key"] = dbg
    return _build_program_inner(dbg, record=False)


def _build_program_inner(dbg=False, record=False):
    nc = bass.Bass("TRN2", target_bir_lowering=False, debug=False,
                   detect_race_conditions=False)

    def din(name, shape, dtype=F32):
        return nc.dram_tensor(name, list(shape), dtype, kind="ExternalInput").ap()

    xT = din("xT", [D, 2048])
    xsw = din("xsw", [2, 128, 32 * 1024])
    wkT = din("wkT", [KVH, 128, 4096])
    wvT = din("wvT", [4, 128, 8192])
    wqT = din("wqT", [H, 128, 4096])
    woT = din("woT", [32, 128, 4096])
    bq_r = din("bq_r", [1, 4096])
    bk_r = din("bk_r", [1, 1024])
    bv_r = din("bv_r", [1, 1024])
    bo_r = din("bo_r", [1, 4096])
    ck_t = din("ck_t", [128, 2048], BF16)
    sk_t = din("sk_t", [128, 2048], BF16)
    tri_m = din("tri_m", [128, 2048], BF16)
    ident = din("ident", [128, 128], BF16)
    identf = din("identf", [128, 128])
    ones_c = din("ones_c", [128, 1], BF16)
    ones_r = din("ones_r", [1, 128], BF16)
    ones512 = din("ones512", [1, 512], BF16)
    ones_rf = din("ones_rf", [1, 128])
    btbl = din("btbl", [128, 24])
    fq = din("fq", [128, 2048])
    fr = din("fr", [64, 48])
    fak = din("fak", [64, 1024])
    favT = din("favT", [8, 128, 4096])
    fvd = din("fvd", [128, 8192])
    fvu = din("fvu", [128, 8192])
    fu1 = din("fu1", [8, 128, 4096])
    fbu1 = din("fbu1", [128, 8])
    fu2 = din("fu2", [128, 8])
    fbu2 = din("fbu2", [1, 1])
    fgw = din("fgw", [1, 1])
    fgb = din("fgb", [1, 1])
    flr = din("flr", [128, 1024])
    crow = din("crow", [1, 8])

    out_d = nc.dram_tensor("out", [D, 1024], F32, kind="ExternalOutput").ap()
    qrope_d = nc.dram_tensor("qrope", [D, 1024], BF16, kind="Internal").ap()
    cc_in = nc.dram_tensor("cc_in", [2, 64], F32, kind="Internal").ap()
    cc_out = nc.dram_tensor("cc_out", [2, 64], F32, kind="Internal").ap()

    dbg_t = {}
    if dbg:
        for name, shape, dty in [
                ("d_k", [128, KVH * 2048], BF16), ("d_v", [128, 16 * 1024], BF16),
                ("d_at", [128, 32 * 1024], BF16), ("d_x2", [128, 32 * 1024], BF16),
                ("d_em", [128, 8 * 1024], BF16), ("d_rows", [4, 1024], F32),
                ("d_gc", [128, 8], F32), ("d_qt", [128, 1024], BF16),
                ("d_avp", [128, 2048], BF16), ("d_x3", [128, 32 * 1024], BF16)]:
            dbg_t[name] = nc.dram_tensor(name, shape, dty, kind="ExternalOutput").ap()

    AL = Alloc(nc, placement=None if record else _PLACEMENT["map"])
    sb = AL.sb

    def sbd(name, shape, dtype, lo, hi):
        return AL.sb(name, shape, dtype, lo, hi + (2 if dbg else 0))

    P = 99
    kbuf = sb("kbuf", [128, KVH * 2048], BF16, 1, 3)
    vbuf = sb("vbuf", [128, 16 * 1024], BF16, 1, 3)
    x12 = sb("x12", [128, 32 * 1024], BF16, 4, 9)
    xcb = sb("xcb", [128, 32 * 1024], BF16, 1, 2)
    abuf = sb("abuf", [128, 32 * 1024], BF16, 3, 4)
    x3b = sb("x3b", [128, 32 * 1024], BF16, 9, 10)
    wop_s = [sb("wop0", [128, 32 * 128], BF16, 2, 4),
             sb("wop1", [128, 32 * 128], BF16, 4, 4)]
    wvd_s = sb("wvd_s", [128, 32 * 256], BF16, 3, 7)
    avpan_s = [sb(f"avpan{i}", [128, 4096], BF16, 3, 5) for i in range(2)]
    avp_s = sbd("avp_s", [128, 8 * 256], BF16, 5, 8)

    tri_s = sb("tri_s", [128, 4 * 512], BF16, 2, 3)
    id_s = sb("id_s", [128, 128], BF16, 0, 8)
    idf_s = sb("idf_s", [128, 128], F32, 0, 8)
    onc_s = sb("onc_s", [128, 1], BF16, 0, 9)
    onr_s = sb("onr_s", [1, 128], BF16, 0, 5)
    onrf_s = sb("onrf_s", [1, 128], F32, 0, 10)
    btbl_s = sb("btbl_s", [128, 24], F32, 0, 3)
    bqb_s = sb("bqb_s", [1, 4096], BF16, 2, 2)
    bkb_s = sb("bkb_s", [1, 1024], BF16, 0, 1)
    bob_s = sb("bob_s", [1, 4096], BF16, 2, 4)
    on5_s = sb("on5_s", [1, 512], BF16, 0, 7)
    bu1_s = sb("bu1_s", [128, 8], F32, 0, 7)
    bvb_s = sb("bvb_s", [1, 1024], BF16, 0, 1)
    wu2_s = sb("wu2_s", [128, 8], BF16, 0, 7)
    gw_s = sb("gw_s", [1, 1], F32, 0, 8)
    gb_s = sb("gb_s", [1, 1], F32, 0, 8)
    lv_s = sb("lv_s", [1, 1024], F32, 5, 8)
    gsum_s = sb("gsum_s", [2, 64], F32, 5, 8)
    bu2b_s = sb("bu2b_s", [1, 1], BF16, 0, 7)
    cst_s = sb("cst_s", [1, 8], F32, 0, 10)

    ck_lo = sb("ck_lo", [128, 1024], BF16, 1, 1)
    sk_lo = sb("sk_lo", [128, 1024], BF16, 1, 1)
    ck_hi = sb("ck_hi", [128, 1024], BF16, 1, 2)
    sk_hi = sb("sk_hi", [128, 1024], BF16, 1, 2)
    kp_s = [sb(f"kp{i}", [128, 4096], BF16, 1, 2) for i in range(2)]
    vp_s = [sb(f"vp{i}", [128, 32 * 256], BF16, 1, 1) for i in range(2)]
    ktmp_s = [sb(f"ktmp{i}", [128, 512], BF16, 1, 2) for i in range(2)]
    kswp_s = [sb(f"kswp{i}", [128, 512], BF16, 1, 2) for i in range(2)]
    qro_s = [sb(f"qro{i}", [128, 1024], BF16, 2, 2) for i in range(2)]
    e_s = [sb(f"e{i}", [128, 512], BF16, 3, 3) for i in range(6)]
    rr_s = [sb(f"rr{i}", [128, 512], BF16, 3, 3) for i in range(2)]
    qin_s = [sb(f"qin{i}", [128, 1024], BF16, 3, 3) for i in range(2)]
    rcl_s = sb("rcl_s", [1, 512], F32, 3, 3)
    rcb_s = sb("rcb_s", [1, 512], BF16, 3, 3)
    rcl2_s = sb("rcl2_s", [1, 512], F32, 3, 3)
    rcb2_s = sb("rcb2_s", [1, 512], BF16, 3, 3)
    xr_s = [sb(f"xr{i}", [128, 512], BF16, 4, 4) for i in range(3)]
    ob_s = [sb(f"ob{i}", [128, 512], BF16, 4, 4) for i in range(2)]
    sq_s = [sb(f"sq{i}", [128, 512], BF16, 4, 4) for i in range(2)]
    srow = sb("srow", [1, 1024], F32, 5, 5)
    sqrow = sb("sqrow", [1, 1024], F32, 5, 5)
    nm_r = sb("nm_r", [1, 1024], F32, 5, 5)
    ex2_r = sb("ex2_r", [1, 1024], F32, 5, 5)
    var_r = sbd("var_r", [1, 1024], F32, 5, 5)
    l1_r = sb("l1_r", [1, 1024], F32, 5, 5)
    rstd_r = sb("rstd_r", [1, 1024], F32, 5, 5)
    nmr_r = sb("nmr_r", [1, 1024], F32, 5, 5)
    rstdb_r = sb("rstdb_r", [1, 1024], BF16, 5, 5)
    nmrb_r = sb("nmrb_r", [1, 1024], BF16, 5, 5)
    rep_a = sb("rep_a", [128, 1024], BF16, 5, 5)
    rep_b = sb("rep_b", [128, 1024], BF16, 5, 5)
    ntmp_s = [sb(f"ntmp{i}", [128, 1024], BF16, 5, 5) for i in range(2)]
    ei_r = sb("ei_r", [1, 1024], F32, 5, 5)
    vu_r = sb("vu_r", [1, 1024], F32, 5, 5)
    pr_r = sb("pr_r", [1, 1], F32, 5, 5)
    ccs_s = sb("ccs_s", [2, 64], F32, 5, 5)
    wqp_s = sb("wqp_s", [128, 2048], BF16, 4, 6)
    wr_s = sb("wr_s", [64, 48], BF16, 4, 6)
    akt_s = sb("akt_s", [64, 1024], BF16, 4, 6)
    flr_sb = sb("flr_sb", [128, 1024], F32, 5, 6)
    qt_s = sbd("qt_s", [128, 1024], BF16, 6, 6)
    rq_s = sb("rq_s", [128, 1024], BF16, 6, 6)
    rk_s = sb("rk_s", [128, 1024], BF16, 6, 6)
    sc_f = [sb(f"scf{i}", [128, 1024], F32, 6, 6) for i in range(2)]
    s2_f = [sb(f"s2f{i}", [128, 1024], F32, 6, 6) for i in range(2)]
    m8_s = sb("m8_s", [128, 8], F32, 6, 6)
    mx_s = sb("mx_s", [128, 1], F32, 6, 6)
    nmx_s = sb("nmx_s", [128, 1], F32, 6, 6)
    e2_s = sb("e2_s", [128, 1024], BF16, 6, 6)
    mkb_s = sb("mkb_s", [128, 1024], BF16, 6, 6)
    emb_s = sbd("emb_s", [128, 8 * 1024], BF16, 6, 8)
    rw_s = sb("rw_s", [128, 8], F32, 6, 8)
    u1p_s = [sb(f"u1p{i}", [128, 4096], BF16, 5, 7) for i in range(2)]
    z1g_s = sb("z1g_s", [128, 8 * 1024], BF16, 6, 7)
    lrn_s = sb("lrn_s", [1, 1024], F32, 6, 8)
    sig_s = sbd("sig_s", [1, 1024], F32, 6, 8)
    g0_r = sb("g0_r", [1, 1024], F32, 6, 8)
    ge_r = sb("ge_r", [1, 1024], F32, 6, 8)
    gate_r = sbd("gate_r", [1, 1024], F32, 6, 8)
    gcol_s = sbd("gcol_s", [128, 8], F32, 8, 8)
    wtb_s = sb("wtb_s", [128, 8 * 1024], BF16, 8, 9)
    wf_s = sb("wf_s", [128, 8 * 1024], F32, 8, 8)
    avd_s = sb("avd_s", [128, 2 * 1024], BF16, 8, 9)
    wvu_s = sb("wvu_s", [128, 2 * 4096], BF16, 7, 9)
    yb_s = [sb(f"yb{i}", [128, 512], BF16, 9, 9) for i in range(2)]
    sq2_s = [sb(f"sq2{i}", [128, 512], BF16, 9, 9) for i in range(2)]
    sum2_r = sb("sum2_r", [1, 1024], F32, 10, 10)
    sq2r_r = sb("sq2r_r", [1, 1024], F32, 10, 10)
    nm2_r = sb("nm2_r", [1, 1024], F32, 10, 10)
    ex22_r = sb("ex22_r", [1, 1024], F32, 10, 10)
    var2_r = sb("var2_r", [1, 1024], F32, 10, 10)
    l2_r = sb("l2_r", [1, 1024], F32, 10, 10)
    rstd2_r = sb("rstd2_r", [1, 1024], F32, 10, 10)
    nmr2_r = sb("nmr2_r", [1, 1024], F32, 10, 10)
    rep2a = sb("rep2a", [128, 1024], F32, 10, 10)
    rep2b = sb("rep2b", [128, 1024], F32, 10, 10)
    outb_s = [sb(f"outb{i}", [128, 1024], F32, 10, 10) for i in range(4)]

    PA = nc.alloc_psum_tensor("PA", [128, 512], F32)
    PB = nc.alloc_psum_tensor("PB", [128, 512], F32)
    PC = nc.alloc_psum_tensor("PC", [128, 512], F32)
    PD = nc.alloc_psum_tensor("PD", [128, 512], F32)
    PE_ = nc.alloc_psum_tensor("PE", [128, 512], F32)
    PF = nc.alloc_psum_tensor("PF", [128, 512], F32)
    PG = nc.alloc_psum_tensor("PG", [128, 512], F32)
    PH = nc.alloc_psum_tensor("PH", [128, 512], F32)

    semnames = ["gp", "sgc", "gx", "gk0", "gk1", "gv0", "gv1", "gr0", "gr1", "gr2", "gf", "ssc", "sq0", "sq1", "so0", "so1", "sp", "sc", "sv", "scc"]
    sems = {}
    for s in semnames:
        sems[s] = nc.semaphore(s).__enter__()

    S = Sched(semnames)
    S.sems = sems

    def dma(engq, sem, out_ap, in_ap):
        S.wait(engq, sem, S.cnt[sem])
        def f(e, _o=out_ap, _i=in_ap, _s=sem):
            e.dma_start(out=_o, in_=_i).then_inc(sems[_s], 16)
        S.emit(engq, f)
        return S.bump(sem, 16)

    def mm(out_ap, lhsT, rhs, start, stop, inc=False):
        def f(e, _o=out_ap, _l=lhsT, _r=rhs, _st=start, _sp=stop, _i=inc):
            ins = e.matmul(_o, _l, _r, start=_st, stop=_sp, skip_group_check=True)
            if _i:
                ins.then_inc(sems["sp"], 1)
        S.emit("tensor", f)
        return S.bump("sp", 1) if inc else None

    def tr(out_ap, in_ap, ident_ap=None, inc=False):
        def f(e, _o=out_ap, _i=in_ap, _id=ident_ap, _n=inc):
            ins = e.transpose(_o, _i, _id if _id is not None else id_s[:])
            if _n:
                ins.then_inc(sems["sp"], 1)
        S.emit("tensor", f)
        return S.bump("sp", 1) if inc else None

    def act(out_ap, in_ap, func, bias=0.0, scale=1.0, accum=None, inc=True):
        def f(e, _o=out_ap, _i=in_ap, _f=func, _b=bias, _s=scale, _a=accum, _n=inc):
            ins = e.activation(_o, _i, _f, bias=_b, scale=_s, accum_out=_a)
            if _n:
                ins.then_inc(sems["sc"], 1)
        S.emit("scalar", f)
        return S.bump("sc", 1) if inc else None

    def vec(opname, *args, inc=True, **kw):
        def f(e, _op=opname, _a=args, _k=kw, _n=inc):
            ins = getattr(e, _op)(*_a, **_k)
            if _n:
                ins.then_inc(sems["sv"], 1)
        S.emit("vector", f)
        return S.bump("sv", 1) if inc else None

    c_ld = 0
    for dst, src in [
        (id_s[:], ident), (idf_s[:], identf), (onc_s[:], ones_c), (onr_s[:], ones_r),
        (onrf_s[:], ones_rf), (btbl_s[:], btbl),
        (on5_s[:], ones512), (bu1_s[:], fbu1), (cst_s[:], crow),
        (gw_s[:], fgw), (gb_s[:], fgb),
        (ck_lo[:], ck_t[:, 0:1024]), (sk_lo[:], sk_t[:, 0:1024]),
        (ck_hi[:], ck_t[:, 1024:2048]), (sk_hi[:], sk_t[:, 1024:2048]),
    ]:
        c_ld = dma("sync", "ssc", dst, src)
    dma("gpsimd", "sgc", wu2_s[:], fu2)
    dma("gpsimd", "sgc", bkb_s[:], bk_r)
    dma("gpsimd", "sgc", bvb_s[:], bv_r)
    g_cst = dma("gpsimd", "sgc", bu2b_s[:], fbu2)
    v_bvb = 0

    S.stop_mark(0)
    ev_xc = [0, 0]
    x_used_pe = 0
    kp_load = [0, 0]
    vp_load = [0, 0]
    kp_free_pe = [0, 0]
    vp_free_pe = [0, 0]
    ktmp_free_v = [0, 0]
    psAB_free_sc = [0, 0]

    S.wait("tensor", "ssc", c_ld)
    S.wait("scalar", "ssc", c_ld)
    for ch in (0, 1):
        if ch == 1:
            S.wait("gpsimd", "sp", x_used_pe)
        ev_xc[ch] = dma("gpsimd", "gx", xcb[:], xsw[ch])
        for g in range(KVH):
            buf = g % 2
            S.wait("gpsimd", "sp", kp_free_pe[buf])
            kp_load[buf] = dma("gpsimd", f"gk{buf}", kp_s[buf][:], wkT[g])
            for qtc in (0, 1):
                ps = PA if qtc == 0 else PB
                S.wait("tensor", f"gk{buf}", kp_load[buf])
                S.wait("tensor", "gx", ev_xc[ch])
                S.wait("tensor", "sc", psAB_free_sc[qtc])
                S.wait("tensor", "sgc", g_cst)
                for k in range(32):
                    mm(ps[:], kp_s[buf][:, k * 128:(k + 1) * 128],
                       xcb[:, k * 1024 + qtc * 512: k * 1024 + qtc * 512 + 512],
                       start=(k == 0), stop=False)
                mm(ps[:], bkb_s[0:1, g * 128:(g + 1) * 128], on5_s[0:1, :],
                   start=False, stop=True, inc=True)
                x_used_pe = kp_free_pe[buf] = S.cnt["sp"]
                tb = qtc
                S.wait("scalar", "sp", S.cnt["sp"])
                S.wait("scalar", "sv", ktmp_free_v[tb])
                psAB_free_sc[qtc] = act(ktmp_s[tb][:], ps[:], AF.Copy)
                cs = ch * 1024 + qtc * 512
                ctab = ck_lo if ch == 0 else ck_hi
                stab = sk_lo if ch == 0 else sk_hi
                co = qtc * 512
                dst = kbuf[:, g * 2048 + cs: g * 2048 + cs + 512]
                S.wait("vector", "sc", S.cnt["sc"])
                S.wait("vector", "ssc", c_ld)
                vec("tensor_copy", kswp_s[tb][0:64, :], ktmp_s[tb][64:128, :])
                vec("tensor_copy", kswp_s[tb][64:128, :], ktmp_s[tb][0:64, :])
                vec("tensor_tensor", dst, ktmp_s[tb][:], ctab[:, co:co + 512], ALU.mult)
                vec("tensor_tensor", kswp_s[tb][:], kswp_s[tb][:], stab[:, co:co + 512],
                    ALU.mult)
                ktmp_free_v[tb] = vec("tensor_tensor", dst, dst, kswp_s[tb][:], ALU.add)
        for vp in range(4):
            buf = vp % 2
            S.wait("gpsimd", "sp", vp_free_pe[buf])
            vp_load[buf] = dma("gpsimd", f"gv{buf}", vp_s[buf][:], wvT[vp])
            for sl in range(8):
                slot = ch * 8 + sl
                ps = PA if sl % 2 == 0 else PB
                S.wait("tensor", f"gv{buf}", vp_load[buf])
                S.wait("tensor", "gx", ev_xc[ch])
                S.wait("tensor", "sgc", g_cst)
                S.wait("tensor", "sc", psAB_free_sc[sl % 2])
                for k in range(32):
                    mm(ps[:, 0:256], xcb[:, k * 1024 + sl * 128: k * 1024 + sl * 128 + 128],
                       vp_s[buf][:, k * 256:(k + 1) * 256],
                       start=(k == 0), stop=False)
                mm(ps[:, 0:256], onr_s[0:1, :], bvb_s[0:1, vp * 256:(vp + 1) * 256],
                   start=False, stop=True, inc=True)
                x_used_pe = vp_free_pe[buf] = S.cnt["sp"]
                S.wait("scalar", "sp", S.cnt["sp"])
                psAB_free_sc[sl % 2] = act(
                    vbuf[:, slot * 1024 + vp * 256: slot * 1024 + vp * 256 + 256],
                    ps[:, 0:256], AF.Copy)

    if dbg:
        S.wait("sync", "sv", S.cnt["sv"])
        S.wait("sync", "sc", S.cnt["sc"])
        dma("sync", "so0", dbg_t["d_k"], kbuf[:])
        dma("sync", "so0", dbg_t["d_v"], vbuf[:])

    S.stop_mark(1)
    ph1_pe = S.cnt["sp"]
    S.wait("gpsimd", "sp", ph1_pe)
    g_cst = dma("gpsimd", "sgc", bqb_s[:], bq_r)
    S.wait("sync", "sp", ph1_pe)
    c_ld = dma("sync", "ssc", tri_s[:], tri_m)

    qp_load = [0, 0]
    qp_free_pe = [0, 0]
    qro_free_st = [0, 0]
    qro_v = [0, 0]
    qst_ev = {}
    for h in range(H):
        buf = h % 2
        S.wait("gpsimd", "sp", qp_free_pe[buf])
        qp_load[buf] = dma("gpsimd", f"gk{buf}", kp_s[buf][:], wqT[h])
        for qtc in (0, 1):
            ps = PA if qtc == 0 else PB
            S.wait("tensor", f"gk{buf}", qp_load[buf])
            S.wait("tensor", "gx", ev_xc[1])
            S.wait("tensor", "sgc", g_cst)
            S.wait("tensor", "sc", psAB_free_sc[qtc])
            for k in range(32):
                mm(ps[:], kp_s[buf][:, k * 128:(k + 1) * 128],
                   xcb[:, k * 1024 + qtc * 512: k * 1024 + qtc * 512 + 512],
                   start=(k == 0), stop=False)
            mm(ps[:], bqb_s[0:1, h * 128:(h + 1) * 128], on5_s[0:1, :],
               start=False, stop=True, inc=True)
            qp_free_pe[buf] = S.cnt["sp"]
            tb = qtc
            S.wait("scalar", "sp", S.cnt["sp"])
            S.wait("scalar", "sv", ktmp_free_v[tb])
            psAB_free_sc[qtc] = act(ktmp_s[tb][:], ps[:], AF.Copy)
            co = qtc * 512
            dst = qro_s[buf][:, qtc * 512: qtc * 512 + 512]
            S.wait("vector", "sc", S.cnt["sc"])
            S.wait("vector", f"so{buf}", qro_free_st[buf])
            vec("tensor_copy", kswp_s[tb][0:64, :], ktmp_s[tb][64:128, :])
            vec("tensor_copy", kswp_s[tb][64:128, :], ktmp_s[tb][0:64, :])
            vec("tensor_tensor", dst, ktmp_s[tb][:], ck_hi[:, co:co + 512], ALU.mult)
            vec("tensor_tensor", kswp_s[tb][:], kswp_s[tb][:], sk_hi[:, co:co + 512],
                ALU.mult)
            ktmp_free_v[tb] = vec("tensor_tensor", dst, dst, kswp_s[tb][:], ALU.add)
            qro_v[buf] = ktmp_free_v[tb]
        S.wait("sync", "sv", qro_v[buf])
        qro_free_st[buf] = dma("sync", f"so{buf}", qrope_d[h * 128:(h + 1) * 128, :],
                               qro_s[buf][:])
        qst_ev[h] = qro_free_st[buf]
    ev_q_done_st = [S.cnt["so0"], S.cnt["so1"]]

    S.stop_mark(2)
    GRAN = {0: [(sg_, sg_ + 4) for sg_ in range(8)],
            1: [(sg_, sg_) for sg_ in range(16)]}
    qin_ld = [0, 0]
    qin_free_pe = [0, 0]
    pc_free_v = [0, 0]
    pd_free_sc = [0, 0]
    pe_free_sc = 0
    rr_free_v = [0, 0]
    exp_on_ps = [0, 0, 0]
    rcl2 = [rcl_s, rcl2_s]
    rcb2 = [rcb_s, rcb2_s]
    pend_tail = []
    q_evac_done = max(psAB_free_sc)
    S.wait("tensor", "ssc", c_ld)
    S.wait("gpsimd", "sp", S.cnt["sp"])
    g_vd = dma("gpsimd", "gf", wvd_s[:], fvd)
    avp_load = [0, 0]
    avp_load[0] = dma("gpsimd", "gk0", avpan_s[0][:], favT[0])
    avp_load[1] = dma("gpsimd", "gk1", avpan_s[1][:], favT[1])

    def flush_tail():
        while pend_tail:
            pend_tail.pop(0)()

    for h in range(H):
        buf = h % 2
        g = h // GROUP
        if h == 0:
            S.wait("tensor", "sc", q_evac_done)
        S.wait("sync", f"so{buf}", qst_ev[h])
        S.wait("sync", "sp", qin_free_pe[buf])
        qin_ld[buf] = dma("sync", f"sq{buf}", qin_s[buf][:],
                          qrope_d[h * 128:(h + 1) * 128, :])
        for c in (0, 1):
            grans = GRAN[c]
            nsl = len(grans)
            pav = PC if c == 0 else PF
            pde = PD if c == 0 else PG
            sc_grp = {}

            def emit_scores(gi, slot):
                ps = (PA, PB, PH)[gi % 3]
                S.wait("tensor", f"sq{buf}", qin_ld[buf])
                S.wait("tensor", "sc", exp_on_ps[gi % 3])
                diag = gi >= nsl - 4
                mm(ps[:], kbuf[:, g * 2048 + slot * 128: g * 2048 + slot * 128 + 128],
                   qin_s[buf][:, c * 512:(c + 1) * 512],
                   start=True, stop=not diag, inc=not diag)
                if diag:
                    j = gi - (nsl - 4)
                    mm(ps[:], id_s[:], tri_s[:, j * 512:(j + 1) * 512],
                       start=False, stop=True, inc=True)
                sc_grp[gi] = S.cnt["sp"]

            emit_scores(*grans[0])
            flush_tail()
            if nsl > 1:
                emit_scores(*grans[1])
            exp_ev = {}
            pend_den = []
            for gi, slot in grans:
                ps = (PA, PB, PH)[gi % 3]
                eb = e_s[gi % 6]
                gidx = gi if c == 0 else 8 + gi
                S.wait("scalar", "sp", sc_grp[gi])
                exp_ev[gi] = exp_on_ps[gi % 3] = act(
                    eb[:], ps[:], AF.Exp,
                    bias=btbl_s[:, gidx:gidx + 1], scale=SC_ATT)
                if gi + 2 < nsl:
                    emit_scores(*grans[gi + 2])
                S.wait("tensor", "sc", exp_ev[gi])
                if gi == 0:
                    S.wait("tensor", "sc", pd_free_sc[c])
                    S.wait("tensor", "sv", pc_free_v[c])
                mm(pav[:], vbuf[:, slot * 1024 + g * 128: slot * 1024 + g * 128 + 128],
                   eb[:], start=(gi == 0), stop=(gi == nsl - 1), inc=(gi == nsl - 1))
                pend_den.append(gi)
                if len(pend_den) == 4 or gi == nsl - 1:
                    for dj in pend_den:
                        mm(pde[0:1, :], onc_s[:], e_s[dj % 6][:],
                           start=(dj == 0), stop=(dj == nsl - 1),
                           inc=(dj == nsl - 1))
                    pend_den = []
            av_grp = S.cnt["sp"]

            def emit_tail(h=h, c=c, av_grp=av_grp, pav=pav, pde=pde):
                nonlocal pe_free_sc
                S.wait("scalar", "sp", av_grp)
                act(rcl2[c][:], pde[0:1, :], AF.Ln)
                pd_free_sc[c] = act(rcb2[c][:], rcl2[c][:], AF.Exp, scale=-1.0)
                S.wait("tensor", "sc", max(pd_free_sc[c], pe_free_sc))
                mm(PE_[:], onr_s[0:1, :], rcb2[c][0:1, :], start=True, stop=True,
                   inc=True)
                rep_grp = S.cnt["sp"]
                S.wait("scalar", "sp", rep_grp)
                S.wait("scalar", "sv", rr_free_v[c])
                pe_free_sc = act(rr_s[c][:], PE_[:], AF.Copy)
                S.wait("vector", "sc", pe_free_sc)
                S.wait("vector", "sp", av_grp)
                pc_free_v[c] = vec("tensor_tensor",
                                   abuf[:, h * 1024 + c * 512: h * 1024 + c * 512 + 512],
                                   pav[:], rr_s[c][:], ALU.mult)
                rr_free_v[c] = pc_free_v[c]

            pend_tail.append(emit_tail)
        qin_free_pe[buf] = S.cnt["sp"]
    flush_tail()
    attn_done_v = S.cnt["sv"]

    if dbg:
        S.wait("sync", "sv", attn_done_v)
        dma("sync", "so0", dbg_t["d_at"], abuf[:])

    S.stop_mark(3)
    g_bob = dma("gpsimd", "gf", bob_s[:], bo_r)
    pend_stats = []
    wo_load = [0, 0]
    wo_free_pe = [0, 0]
    xr_ld = [0, 0, 0]
    xr_free_v = [0, 0, 0]
    ob_free_v = [0, 0]
    sq_free_pe = [0, 0]
    sq_free_sc = [0, 0]
    gcnt = 0
    for p in range(32):
        buf = p % 2
        S.wait("gpsimd", "sp", wo_free_pe[buf])
        if p == 1:
            S.wait("gpsimd", "sv", attn_done_v)
        wo_load[buf] = dma("gpsimd", f"gv{buf}", wop_s[buf][:], woT[p])
        if True:
            dtile = p
            for qtc in (0, 1):
                ps = PA if gcnt % 2 == 0 else PB
                xb = gcnt % 3
                if gcnt == 0:
                    S.wait("gpsimd", "sv", attn_done_v)
                S.wait("gpsimd", "sv", xr_free_v[xb])
                xr_ld[xb] = dma("gpsimd", f"gr{xb}", xr_s[xb][:],
                                xT[dtile * 128:(dtile + 1) * 128,
                                   1024 + qtc * 512: 1024 + qtc * 512 + 512])
                if len(pend_stats) > 1:
                    pend_stats.pop(0)()
                S.wait("tensor", f"gv{buf}", wo_load[buf])
                S.wait("tensor", "gf", g_bob)
                if gcnt == 0:
                    S.wait("tensor", "sv", attn_done_v)
                S.wait("tensor", "sc", psAB_free_sc[gcnt % 2])
                for k in range(32):
                    mm(ps[:], wop_s[buf][:, k * 128:(k + 1) * 128],
                       abuf[:, k * 1024 + qtc * 512: k * 1024 + qtc * 512 + 512],
                       start=(k == 0), stop=False)
                mm(ps[:], bob_s[0:1, dtile * 128:(dtile + 1) * 128], on5_s[0:1, :],
                   start=False, stop=True, inc=True)
                wo_free_pe[buf] = S.cnt["sp"]
                S.wait("scalar", "sp", S.cnt["sp"])
                S.wait("scalar", "sv", ob_free_v[gcnt % 2])
                psAB_free_sc[gcnt % 2] = act(ob_s[gcnt % 2][:], ps[:], AF.Copy)
                x1sl = x12[:, dtile * 1024 + qtc * 512: dtile * 1024 + qtc * 512 + 512]
                S.wait("vector", "sc", S.cnt["sc"])
                S.wait("vector", f"gr{xb}", xr_ld[xb])
                ob_free_v[gcnt % 2] = vec("tensor_tensor", x1sl, ob_s[gcnt % 2][:],
                                          xr_s[xb][:], ALU.add)
                xr_free_v[xb] = ob_free_v[gcnt % 2]
                S.wait("scalar", "sv", S.cnt["sv"])
                S.wait("scalar", "sp", sq_free_pe[gcnt % 2])
                sq_free_sc[gcnt % 2] = act(sq_s[gcnt % 2][:], x1sl, AF.Square)
                sc_now, sv_now = S.cnt["sc"], S.cnt["sv"]

                def emit_stats(x1sl=x1sl, sqb=sq_s[gcnt % 2], qtc=qtc, gcnt=gcnt,
                               sc_now=sc_now, sv_now=sv_now,
                               first=(p == 0),
                               last=(p == 31)):
                    S.wait("tensor", "sc", sc_now)
                    S.wait("tensor", "sv", sv_now)
                    pss = PC if qtc == 0 else PD
                    psq = PE_ if qtc == 0 else PF
                    mm(pss[0:1, :], onc_s[:], x1sl, start=first, stop=last)
                    mm(psq[0:1, :], onc_s[:], sqb[:], start=first, stop=last,
                       inc=True)
                    sq_free_pe[gcnt % 2] = S.cnt["sp"]

                pend_stats.append(emit_stats)
                gcnt += 1
    while pend_stats:
        pend_stats.pop(0)()
    stats_pe = S.cnt["sp"]

    S.stop_mark(4)
    u1_load = [0, 0]
    S.wait("gpsimd", "sp", stats_pe)
    u1_load[0] = dma("gpsimd", "gv0", u1p_s[0][:], fu1[0])
    u1_load[1] = dma("gpsimd", "gv1", u1p_s[1][:], fu1[1])
    avp_sc = 0
    avp_free_pe = [0, 0]
    for st in range(8):
        buf = st % 2
        S.wait("tensor", f"gk{buf}", avp_load[buf])
        S.wait("tensor", "gf", g_vd)
        S.wait("tensor", "sc", avp_sc)
        for k in range(32):
            mm(PG[:, 0:256], avpan_s[buf][:, k * 128:(k + 1) * 128],
               wvd_s[:, k * 256:(k + 1) * 256], start=(k == 0), stop=(k == 31),
               inc=(k == 31))
        avp_free_pe[buf] = S.cnt["sp"]
        if st + 2 < 8:
            S.wait("gpsimd", "sp", avp_free_pe[buf])
            avp_load[buf] = dma("gpsimd", f"gk{buf}", avpan_s[buf][:], favT[st + 2])
        S.wait("scalar", "sp", S.cnt["sp"])
        avp_sc = act(avp_s[:, st * 256:(st + 1) * 256], PG[:, 0:256], AF.Copy)
    avp_done = S.cnt["sc"]

    S.wait("vector", "sp", stats_pe)
    vec("tensor_copy", srow[0:1, 0:512], PC[0:1, :])
    vec("tensor_copy", srow[0:1, 512:1024], PD[0:1, :])
    vec("tensor_copy", sqrow[0:1, 0:512], PE_[0:1, :])
    v_stats = vec("tensor_copy", sqrow[0:1, 512:1024], PF[0:1, :])
    vec("tensor_scalar", nm_r[:], srow[:], -1.0 / D, None, ALU.mult)
    vec("tensor_scalar", ex2_r[:], sqrow[:], 1.0 / D, None, ALU.mult)
    vec("tensor_tensor", srow[:], nm_r[:], nm_r[:], ALU.mult)
    v_var = vec("tensor_tensor", var_r[:], ex2_r[:], srow[:], ALU.subtract)
    S.wait("scalar", "sv", v_var)
    act(l1_r[:], var_r[:], AF.Ln, bias=cst_s[0:1, 0:1])
    sc_rstd = act(rstd_r[:], l1_r[:], AF.Exp, scale=-0.5)
    sc_ei = act(ei_r[:], l1_r[:], AF.Exp, scale=-1.0)
    S.wait("vector", "sc", sc_rstd)
    vec("tensor_tensor", nmr_r[:], nm_r[:], rstd_r[:], ALU.mult)
    vec("tensor_copy", rstdb_r[:], rstd_r[:])
    v_rows = vec("tensor_copy", nmrb_r[:], nmr_r[:])
    S.wait("vector", "sc", sc_ei)
    v_vu = vec("tensor_tensor", vu_r[:], var_r[:], ei_r[:], ALU.mult)
    S.wait("scalar", "sv", v_vu)
    sc_lv = act(lv_s[:], vu_r[:], AF.Ln, bias=cst_s[0:1, 1:2])
    S.wait("vector", "sc", sc_lv)
    vec("memset", ccs_s[:], 0.0)
    vec("reduce_sum", pr_r[:], lv_s[:], AX)
    v_cc = vec("tensor_copy", ccs_s[0:1, 0:1], pr_r[:])
    g_wq = dma("gpsimd", "gf", wqp_s[:], fq)
    g_wr = dma("gpsimd", "gf", wr_s[:], fr)
    g_ak = dma("gpsimd", "gf", akt_s[:], fak)
    S.wait("gpsimd", "sv", v_cc)
    g_ccin = dma("gpsimd", "gf", cc_in, ccs_s[:])
    S.wait("gpsimd", "gf", g_ccin)

    def f_cc(e):
        e.collective_compute("AllReduce", ALU.add, replica_groups=[list(range(8))],
                             ins=[cc_in.opt()], outs=[cc_out.opt()]).then_inc(
            sems["scc"], 1)
    S.emit("gpsimd", f_cc)
    S.bump("scc", 1)
    S.wait("gpsimd", "scc", 1)
    g_ccout = dma("gpsimd", "gf", gsum_s[:], cc_out)

    S.wait("tensor", "sv", v_rows)
    mm(PA[:], onr_s[0:1, :], rstdb_r[0:1, 0:512], start=True, stop=True)
    mm(PB[:], onr_s[0:1, :], rstdb_r[0:1, 512:1024], start=True, stop=True)
    mm(PC[:], onr_s[0:1, :], nmrb_r[0:1, 0:512], start=True, stop=True)
    rep_pe = mm(PD[:], onr_s[0:1, :], nmrb_r[0:1, 512:1024], start=True, stop=True,
                inc=True)
    S.wait("scalar", "sp", rep_pe)
    act(rep_a[:, 0:512], PA[:], AF.Copy)
    act(rep_a[:, 512:1024], PB[:], AF.Copy)
    act(rep_b[:, 0:512], PC[:], AF.Copy)
    rep_sc = act(rep_b[:, 512:1024], PD[:], AF.Copy)
    psAB_free_sc = [rep_sc, rep_sc]
    S.wait("vector", "sc", rep_sc)
    ntmp_free = [0, 0]
    x2_half_v = [0, 0]
    for qtc in (0, 1):
        for dt_i in range(32):
            tb = dt_i % 2
            sl = slice(dt_i * 1024 + qtc * 512, dt_i * 1024 + qtc * 512 + 512)
            S.wait("vector", "sv", ntmp_free[tb])
            vec("tensor_tensor", ntmp_s[tb][:, 0:512], x12[:, sl],
                rep_a[:, qtc * 512:(qtc + 1) * 512], ALU.mult)
            ntmp_free[tb] = vec("tensor_tensor", x12[:, sl], ntmp_s[tb][:, 0:512],
                                rep_b[:, qtc * 512:(qtc + 1) * 512], ALU.add)
        x2_half_v[qtc] = S.cnt["sv"]
    x2_done_v = S.cnt["sv"]

    if dbg:
        S.wait("sync", "sv", x2_done_v)
        dma("sync", "so0", dbg_t["d_x2"], x12[:])
        S.wait("sync", "sc", sc_lv)
        dma("sync", "so0", dbg_t["d_rows"][0:1, :], lv_s[:])
        dma("sync", "so0", dbg_t["d_rows"][1:2, :], var_r[:])

    S.stop_mark(5)
    S.wait("sync", "sp", stats_pe)
    l_lr = dma("sync", "ssc", flr_sb[:], flr)
    for qtc in (0, 1):
        ps = PA if qtc == 0 else PB
        S.wait("tensor", "gf", g_wq)
        S.wait("tensor", "sv", x2_half_v[qtc])
        S.wait("tensor", "sc", psAB_free_sc[qtc])
        S.wait("tensor", "sc", max(exp_on_ps))
        for k in range(32):
            mm(ps[0:64, :], wqp_s[:, k * 64:(k + 1) * 64],
               x12[:, k * 1024 + qtc * 512: k * 1024 + qtc * 512 + 512],
               start=(k == 0), stop=(k == 31), inc=(k == 31))
        S.wait("scalar", "sp", S.cnt["sp"])
        psAB_free_sc[qtc] = act(qt_s[0:64, qtc * 512:(qtc + 1) * 512], ps[0:64, :],
                                AF.Copy)
    qt_sc = S.cnt["sc"]
    for dst, rhs, ld in [(rq_s, qt_s, qt_sc), (rk_s, akt_s, 0)]:
        for qtc in (0, 1):
            ps = PA if qtc == 0 else PB
            S.wait("tensor", "gf", max(g_wr, g_ak))
            S.wait("tensor", "sc", max(psAB_free_sc[qtc], ld))
            mm(ps[0:48, :], wr_s[0:64, :], rhs[0:64, qtc * 512:(qtc + 1) * 512],
               start=True, stop=True, inc=True)
            S.wait("scalar", "sp", S.cnt["sp"])
            psAB_free_sc[qtc] = act(dst[0:48, qtc * 512:(qtc + 1) * 512], ps[0:48, :],
                                    AF.Copy)
    rqrk_sc = S.cnt["sc"]
    tile_ps_free_v = 0
    rt_first = True

    def emit_rtile(tt):
        nonlocal tile_ps_free_v, rt_first
        fb = tt % 2
        S.wait("tensor", "sc", rqrk_sc)
        if rt_first:
            S.wait("tensor", "sc", pe_free_sc)
            S.wait("tensor", "sv", v_stats)
            rt_first = False
        S.wait("tensor", "sv", tile_ps_free_v)
        mm(PC[:], rq_s[0:48, tt * 128:(tt + 1) * 128], rk_s[0:48, 0:512],
           start=True, stop=True)
        mm(PD[:], rq_s[0:48, tt * 128:(tt + 1) * 128], rk_s[0:48, 512:1024],
           start=True, stop=True)
        mm(PE_[:], qt_s[0:64, tt * 128:(tt + 1) * 128], akt_s[0:64, 0:512],
           start=True, stop=True)
        rt_pe = mm(PF[:], qt_s[0:64, tt * 128:(tt + 1) * 128], akt_s[0:64, 512:1024],
                   start=True, stop=True, inc=True)
        S.wait("vector", "sp", rt_pe)
        S.wait("vector", "ssc", l_lr)
        vec("scalar_tensor_tensor", sc_f[fb][:, 0:512], PC[:], SC_RT,
            flr_sb[:, 0:512], ALU.mult, ALU.add)
        vec("scalar_tensor_tensor", sc_f[fb][:, 512:1024], PD[:], SC_RT,
            flr_sb[:, 512:1024], ALU.mult, ALU.add)
        vec("scalar_tensor_tensor", s2_f[fb][:, 0:512], PE_[:], 0.125,
            flr_sb[:, 0:512], ALU.mult, ALU.add)
        ps_done = vec("scalar_tensor_tensor", s2_f[fb][:, 512:1024], PF[:], 0.125,
                      flr_sb[:, 512:1024], ALU.mult, ALU.add)
        tile_ps_free_v = ps_done
        vec("max", m8_s[:], sc_f[fb][:])
        vec("reduce_max", mx_s[:], s2_f[fb][:], AX)
        vec("tensor_scalar", nmx_s[:], mx_s[:], -1.0, None, ALU.mult)
        v_nmx = S.cnt["sv"]
        S.wait("scalar", "sv", v_nmx)
        sc_e2 = act(e2_s[:], s2_f[fb][:], AF.Exp, bias=nmx_s[:])
        vec("tensor_scalar", mkb_s[:], sc_f[fb][:], m8_s[:, 7:8], None, ALU.is_ge)
        S.wait("vector", "sc", sc_e2)
        vec("tensor_tensor", emb_s[:, tt * 1024:(tt + 1) * 1024], e2_s[:], mkb_s[:],
            ALU.mult)
        vec("reduce_sum", mx_s[:], emb_s[:, tt * 1024:(tt + 1) * 1024], AX)
        vec("reciprocal", rw_s[:, tt:tt + 1], mx_s[:])

    S.stop_mark(6)
    u1_free_pe = [0, 0]
    z1_sc = {}
    for m in range(8):
        buf = m % 2
        if m >= 2:
            S.wait("gpsimd", "sp", u1_free_pe[buf])
            u1_load[buf] = dma("gpsimd", f"gv{buf}", u1p_s[buf][:], fu1[m])
        for qtc in (0, 1):
            ps = PA if qtc == 0 else PB
            S.wait("tensor", f"gv{buf}", u1_load[buf])
            S.wait("tensor", "sv", x2_half_v[qtc])
            if m == 0:
                S.wait("tensor", "sc", max(avp_sc, rqrk_sc))
            S.wait("tensor", "sc", z1_sc.get((m - 1, qtc), 0))
            for k in range(32):
                mm(ps[:], u1p_s[buf][:, k * 128:(k + 1) * 128],
                   x12[:, k * 1024 + qtc * 512: k * 1024 + qtc * 512 + 512],
                   start=(k == 0), stop=(k == 31), inc=(k == 31))
            u1_free_pe[buf] = S.cnt["sp"]
            S.wait("scalar", "sp", S.cnt["sp"])
            z1_sc[(m, qtc)] = act(
                z1g_s[:, m * 1024 + qtc * 512: m * 1024 + qtc * 512 + 512],
                ps[:], AF.Gelu_apprx_tanh, bias=bu1_s[:, m:m + 1])
        emit_rtile(m)
        S.wait("tensor", "sc", z1_sc[(m, 1)])
        if m == 0:
            S.wait("tensor", "sc", avp_done)
        for qtc in (0, 1):
            po = PG[0:1, :] if qtc == 0 else PG[64:65, :]
            mm(po, wu2_s[:, m:m + 1],
               z1g_s[:, m * 1024 + qtc * 512: m * 1024 + qtc * 512 + 512],
               start=(m == 0), stop=False)
            if m == 7:
                mm(po, bu2b_s[0:1, 0:1], on5_s[0:1, :], start=False, stop=True,
                   inc=(qtc == 1))
    lrn_pe = S.cnt["sp"]
    S.wait("scalar", "sp", lrn_pe)
    act(lrn_s[0:1, 0:512], PG[0:1, :], AF.Copy)
    lrn_sc = act(lrn_s[0:1, 512:1024], PG[64:65, :], AF.Copy)
    z1_done = S.cnt["sc"]
    em_done_v = S.cnt["sv"]

    if dbg:
        S.wait("sync", "sv", em_done_v)
        dma("sync", "so0", dbg_t["d_em"], emb_s[:])
        dma("sync", "so0", dbg_t["d_qt"][0:64, :], qt_s[0:64, :])
    S.stop_mark(6.25)
    act(sig_s[:], lrn_s[:], AF.Exp, scale=-1.0)
    act(sig_s[:], sig_s[:], AF.Ln, bias=cst_s[0:1, 1:2])
    sig_done = act(sig_s[:], sig_s[:], AF.Exp, scale=-1.0)

    S.wait("vector", "gf", g_ccout)
    vec("tensor_scalar", gsum_s[0:1, 0:1], gsum_s[0:1, 0:1], 1.0 / 8192.0,
        1e-6, ALU.mult, ALU.add)
    vec("reciprocal", gsum_s[0:1, 1:2], gsum_s[0:1, 0:1])
    S.wait("vector", "sc", sig_done)
    vec("tensor_scalar", g0_r[:], lv_s[:], gsum_s[0:1, 1:2], None, ALU.mult)
    vec("tensor_scalar", ge_r[:], sig_s[:], 2.5, None, ALU.mult)
    vec("scalar_tensor_tensor", g0_r[:], g0_r[:], 0.5, ge_r[:], ALU.mult, ALU.add)
    vec("tensor_scalar", g0_r[:], g0_r[:], 0.0, 5.0, ALU.max, ALU.min)
    vec("tensor_scalar", g0_r[:], g0_r[:], -0.5, 1.0 / 1.5, ALU.add, ALU.mult)
    vec("tensor_scalar", g0_r[:], g0_r[:], 0.0, 1.0, ALU.max, ALU.min)
    v_un = vec("tensor_scalar", g0_r[:], g0_r[:], gw_s[0:1, 0:1], gb_s[0:1, 0:1],
               ALU.mult, ALU.add)
    S.wait("scalar", "sv", v_un)
    act(gate_r[:], g0_r[:], AF.Exp, scale=-1.0)
    act(gate_r[:], gate_r[:], AF.Ln, bias=cst_s[0:1, 1:2])
    sc_g = act(gate_r[:], gate_r[:], AF.Exp, scale=-1.0)
    S.wait("vector", "sc", sc_g)
    vec("tensor_scalar", ge_r[:], gate_r[:], 0.05, None, ALU.is_ge)
    v_gate = vec("tensor_tensor", gate_r[:], gate_r[:], ge_r[:], ALU.mult)
    S.tail_v_gate = v_gate
    S.stop_mark(6.5)

    if dbg:
        S.wait("sync", "sc", avp_done)
        dma("sync", "so0", dbg_t["d_avp"], avp_s[:])
        S.wait("sync", "sc", sig_done)
        dma("sync", "so0", dbg_t["d_rows"][2:3, :], sig_s[:])

    S.stop_mark(7)
    v_gate = S.tail_v_gate
    S.wait("tensor", "sv", v_gate)
    for tt in range(8):
        mm(PE_[:, tt:tt + 1], gate_r[0:1, tt * 128:(tt + 1) * 128], onrf_s[0:1, 0:1],
           start=True, stop=True, inc=(tt == 7))
    S.wait("scalar", "sp", S.cnt["sp"])
    gcol_sc = act(gcol_s[:], PE_[:, 0:8], AF.Copy)
    S.wait("vector", "sc", gcol_sc)
    S.wait("vector", "sv", em_done_v)
    w_v = 0
    for tt in range(8):
        w_v = vec("tensor_scalar", wf_s[:, tt * 1024:(tt + 1) * 1024],
                  emb_s[:, tt * 1024:(tt + 1) * 1024], rw_s[:, tt:tt + 1],
                  gcol_s[:, tt:tt + 1], ALU.mult, ALU.mult)
    S.wait("tensor", "sv", w_v)
    tr_sc = 0
    for st in range(8):
        for half in (0, 1):
            S.wait("tensor", "sc", tr_sc)
            for q4 in range(4):
                tt = half * 4 + q4
                tr(PH[:, q4 * 128:(q4 + 1) * 128],
                   wf_s[:, tt * 1024 + st * 128: tt * 1024 + st * 128 + 128],
                   ident_ap=idf_s[:], inc=(q4 == 3))
            S.wait("scalar", "sp", S.cnt["sp"])
            tr_sc = act(wtb_s[:, st * 1024 + half * 512: st * 1024 + half * 512 + 512],
                        PH[:], AF.Copy)
    avd_sc = {}
    for ct in (0, 1):
        for qtc in (0, 1):
            ps = PA if qtc == 0 else PB
            S.wait("tensor", "sc", max(avp_done, tr_sc))
            for k in range(8):
                mm(ps[:], avp_s[:, k * 256 + ct * 128: k * 256 + ct * 128 + 128],
                   wtb_s[:, k * 1024 + qtc * 512: k * 1024 + qtc * 512 + 512],
                   start=(k == 0), stop=(k == 7), inc=(k == 7))
            S.wait("scalar", "sp", S.cnt["sp"])
            avd_sc[(ct, qtc)] = act(
                avd_s[:, ct * 1024 + qtc * 512: ct * 1024 + qtc * 512 + 512],
                ps[:], AF.Copy)
    avd_done = S.cnt["sc"]

    if dbg:
        S.wait("sync", "sc", gcol_sc)
        dma("sync", "so0", dbg_t["d_gc"], gcol_s[:])
        S.wait("sync", "sv", S.cnt["sv"])
        dma("sync", "so0", dbg_t["d_rows"][3:4, :], gate_r[:])

    S.stop_mark(8)
    S.wait("gpsimd", "sv", em_done_v)
    g_vu = dma("gpsimd", "gf", wvu_s[:], fvu)
    yb_free_v = [0, 0]
    sq2_sc = [0, 0]
    y_sc = [0, 0, 0, 0]
    gcnt = 0
    for dt_i in range(32):
        for qtc in (0, 1):
            ps = (PA, PB, PG, PH)[gcnt % 4]
            if len(pend_stats) > 1:
                pend_stats.pop(0)()
            S.wait("tensor", "gf", g_vu)
            S.wait("tensor", "sc", max(avd_done, y_sc[gcnt % 4]))
            for k in range(2):
                mm(ps[:], wvu_s[:, k * 4096 + dt_i * 128: k * 4096 + dt_i * 128 + 128],
                   avd_s[:, k * 1024 + qtc * 512: k * 1024 + qtc * 512 + 512],
                   start=(k == 0), stop=(k == 1), inc=(k == 1))
            S.wait("scalar", "sp", S.cnt["sp"])
            S.wait("scalar", "sv", yb_free_v[gcnt % 2])
            y_sc[gcnt % 4] = act(yb_s[gcnt % 2][:], ps[:], AF.Copy)
            x3sl = x3b[:, dt_i * 1024 + qtc * 512: dt_i * 1024 + qtc * 512 + 512]
            S.wait("vector", "sc", S.cnt["sc"])
            yb_free_v[gcnt % 2] = vec("tensor_tensor", x3sl, yb_s[gcnt % 2][:],
                                      x12[:, dt_i * 1024 + qtc * 512:
                                          dt_i * 1024 + qtc * 512 + 512], ALU.add)
            S.wait("scalar", "sv", S.cnt["sv"])
            S.wait("scalar", "sp", sq_free_pe[gcnt % 2])
            sq2_sc[gcnt % 2] = act(sq2_s[gcnt % 2][:], x3sl, AF.Square)
            sc_now, sv_now = S.cnt["sc"], S.cnt["sv"]

            def emit_stats2(x3sl=x3sl, sqb=sq2_s[gcnt % 2], qtc=qtc, gcnt=gcnt,
                            sc_now=sc_now, sv_now=sv_now,
                            first=(dt_i == 0), last=(dt_i == 31)):
                S.wait("tensor", "sc", sc_now)
                S.wait("tensor", "sv", sv_now)
                pss = PC if qtc == 0 else PD
                psq = PE_ if qtc == 0 else PF
                mm(pss[0:1, :], onc_s[:], x3sl, start=first, stop=last)
                mm(psq[0:1, :], onc_s[:], sqb[:], start=first, stop=last, inc=True)
                sq_free_pe[gcnt % 2] = S.cnt["sp"]

            pend_stats.append(emit_stats2)
            gcnt += 1
    while pend_stats:
        pend_stats.pop(0)()
    stats2_pe = S.cnt["sp"]

    if dbg:
        S.wait("sync", "sp", stats2_pe)
        dma("sync", "so0", dbg_t["d_x3"], x3b[:])

    S.stop_mark(9)
    S.wait("vector", "sp", stats2_pe)
    vec("tensor_copy", sum2_r[0:1, 0:512], PC[0:1, :])
    vec("tensor_copy", sum2_r[0:1, 512:1024], PD[0:1, :])
    vec("tensor_copy", sq2r_r[0:1, 0:512], PE_[0:1, :])
    vec("tensor_copy", sq2r_r[0:1, 512:1024], PF[0:1, :])
    vec("tensor_scalar", nm2_r[:], sum2_r[:], -1.0 / D, None, ALU.mult)
    vec("tensor_scalar", ex22_r[:], sq2r_r[:], 1.0 / D, None, ALU.mult)
    vec("tensor_tensor", sum2_r[:], nm2_r[:], nm2_r[:], ALU.mult)
    v_var2 = vec("tensor_tensor", var2_r[:], ex22_r[:], sum2_r[:], ALU.subtract)
    S.wait("scalar", "sv", v_var2)
    act(l2_r[:], var2_r[:], AF.Ln, bias=cst_s[0:1, 0:1])
    sc_rstd2 = act(rstd2_r[:], l2_r[:], AF.Exp, scale=-0.5)
    S.wait("vector", "sc", sc_rstd2)
    v_rows2 = vec("tensor_tensor", nmr2_r[:], nm2_r[:], rstd2_r[:], ALU.mult)
    S.wait("tensor", "sv", v_rows2)
    mm(PA[:], onrf_s[0:1, :], rstd2_r[0:1, 0:512], start=True, stop=True)
    mm(PB[:], onrf_s[0:1, :], rstd2_r[0:1, 512:1024], start=True, stop=True)
    mm(PC[:], onrf_s[0:1, :], nmr2_r[0:1, 0:512], start=True, stop=True)
    rep2_pe = mm(PD[:], onrf_s[0:1, :], nmr2_r[0:1, 512:1024], start=True, stop=True,
                 inc=True)
    S.wait("scalar", "sp", rep2_pe)
    act(rep2a[:, 0:512], PA[:], AF.Copy)
    act(rep2a[:, 512:1024], PB[:], AF.Copy)
    act(rep2b[:, 0:512], PC[:], AF.Copy)
    rep2_sc = act(rep2b[:, 512:1024], PD[:], AF.Copy)
    S.wait("vector", "sc", rep2_sc)
    S.wait("gpsimd", "sc", rep2_sc)
    out_free_st = [0, 0, 0, 0]
    for dt_i in range(32):
        tb = dt_i % 4
        gp = (dt_i % 3 == 2)
        eng = "gpsimd" if gp else "vector"
        sem = "gp" if gp else "sv"

        def go(opname, *args, _e=eng, _s=sem, inc=True):
            def f(e, _op=opname, _a=args, _n=inc):
                ins = getattr(e, _op)(*_a)
                if _n:
                    ins.then_inc(sems[_s], 1)
            S.emit(_e, f)
            return S.bump(_s, 1) if inc else None

        S.wait(eng, f"so{tb % 2}", out_free_st[tb])
        go("tensor_tensor", outb_s[tb][:], x3b[:, dt_i * 1024:(dt_i + 1) * 1024],
           rep2a[:], ALU.mult)
        v_o = go("tensor_tensor", outb_s[tb][:], outb_s[tb][:], rep2b[:], ALU.add)
        S.wait("sync", sem, v_o)
        out_free_st[tb] = dma("sync", f"so{tb % 2}",
                              out_d[dt_i * 128:(dt_i + 1) * 128, :], outb_s[tb][:])

    import os
    _pad = int(os.environ.get("PE_PAD", "0"))
    for _ in range(_pad):
        S.emit("tensor", lambda e: e.nop())

    S.wait("sync", "so0", S.cnt["so0"])
    S.wait("sync", "so1", S.cnt["so1"])

    if record:
        return AL.reqs

    with nc.Block() as blk:
        @blk.tensor
        def _(e):
            for f in S.q["tensor"]:
                f(e)

        @blk.vector
        def _(e):
            for f in S.q["vector"]:
                f(e)

        @blk.scalar
        def _(e):
            for f in S.q["scalar"]:
                f(e)

        @blk.gpsimd
        def _(e):
            for f in S.q["gpsimd"]:
                f(e)

        @blk.sync
        def _(e):
            for f in S.q["sync"]:
                f(e)

    return nc




def _prep_inputs(inputs):
    x = np.ascontiguousarray(np.asarray(inputs["L0_attn_x"], np.float32))
    wqkv = np.asarray(inputs["L0_attn_w_qkv"], np.float32)
    bqkv = np.asarray(inputs["L0_attn_b_qkv"], np.float32)
    wo = np.asarray(inputs["L0_attn_w_o"], np.float32)
    bo = np.asarray(inputs["L0_attn_b_o"], np.float32)

    perm = _perm_qkv_rows()
    wp = wqkv[perm]
    bp = bqkv[perm]

    def _swz(panels):
        NP, Dd, C = panels.shape
        return np.ascontiguousarray(
            panels.reshape(NP, 32, 128, C).swapaxes(1, 2).reshape(NP, 128, 32 * C))

    wqT_h = _swz(wp[:4096].reshape(H, 128, D).transpose(0, 2, 1))
    wkT_h = _swz(wp[4096:5120].reshape(KVH, 128, D).transpose(0, 2, 1))
    wvT_h = _swz(wp[5120:6144].reshape(4, 256, D).transpose(0, 2, 1))
    woT_h = _swz(wo.T.reshape(D, 32, 128).transpose(1, 0, 2))
    bq_h = np.ascontiguousarray(bp[:4096].reshape(1, 4096))
    bk_h = np.ascontiguousarray(bp[4096:5120].reshape(1, 1024))
    bv_h = np.ascontiguousarray(bp[5120:6144].reshape(1, 1024))
    bo_h = np.ascontiguousarray(bo.reshape(1, 4096))

    tri_h = np.ascontiguousarray(_tri_masks().transpose(1, 0, 2).reshape(128, 2048))
    id_h = np.eye(128, dtype=bfdt)
    idf_h = np.eye(128, dtype=np.float32)
    ones_c_h = np.ones((128, 1), bfdt)
    ones_r_h = np.ones((1, 128), bfdt)
    ones512_h = np.ones((1, 512), bfdt)
    ones_rf_h = np.ones((1, 128), np.float32)

    fq_h = _swz(np.asarray(inputs["L0_ffn_Wq"], np.float32).T[None])[0]
    fr_h = np.ascontiguousarray(np.asarray(inputs["L0_ffn_Wr"], np.float32).T)
    fak_h = np.ascontiguousarray(np.asarray(inputs["L0_ffn_aux_keys"], np.float32).T)
    favT_h = _swz(np.asarray(inputs["L0_ffn_aux_values"], np.float32)
                  .reshape(8, 128, D).transpose(0, 2, 1))
    fvd_h = _swz(np.asarray(inputs["L0_ffn_Wvd"], np.float32).T[None])[0]
    fvu_t = np.asarray(inputs["L0_ffn_Wvu"], np.float32).T
    fvu_h = np.ascontiguousarray(
        fvu_t.reshape(2, 128, D).swapaxes(0, 1).reshape(128, 2 * D))
    fu1_h = _swz(np.asarray(inputs["L0_ffn_Wu1"], np.float32)
                 .reshape(8, 128, D).transpose(0, 2, 1))
    fbu1_h = np.ascontiguousarray(
        np.asarray(inputs["L0_ffn_bu1"], np.float32).reshape(8, 128).T)
    fu2_h = np.ascontiguousarray(
        np.asarray(inputs["L0_ffn_Wu2"], np.float32).reshape(8, 128).T)
    fbu2_h = np.asarray(inputs["L0_ffn_bu2"], np.float32).reshape(1, 1)
    fgw_h = np.asarray(inputs["L0_ffn_gate_w1"], np.float32).reshape(1, 1)
    fgb_h = np.asarray(inputs["L0_ffn_gate_bias"], np.float32).reshape(1, 1)
    flr_h = np.ascontiguousarray(np.broadcast_to(
        np.asarray(inputs["L0_ffn_log_rel"], np.float32)[None, :], (128, 1024)))

    common = dict(wkT=wkT_h, wvT=wvT_h, wqT=wqT_h, woT=woT_h, bq_r=bq_h, bk_r=bk_h,
                  bv_r=bv_h, bo_r=bo_h, tri_m=tri_h, ident=id_h, identf=idf_h, ones_c=ones_c_h,
                  ones_r=ones_r_h, ones512=ones512_h, ones_rf=ones_rf_h,
                  fq=fq_h, fr=fr_h, fak=fak_h,
                  favT=favT_h, fvd=fvd_h, fvu=fvu_h, fu1=fu1_h, fbu1=fbu1_h,
                  fu2=fu2_h, fbu2=fbu2_h, fgw=fgw_h, fgb=fgb_h, flr=flr_h,
                  crow=np.array([[EPS, 1.0, 0, 0, 0, 0, 0, 0]], np.float32))

    in_maps = []
    for core in range(8):
        b, half = core // 2, core % 2
        stb = _slot_to_block(half)
        xTc = np.zeros((D, 2048), np.float32)
        pos = np.zeros(2048, np.float64)
        for s, bl in enumerate(stb):
            if bl >= 0:
                xTc[:, s * 128:(s + 1) * 128] = x[b, bl * 128:(bl + 1) * 128, :].T
                pos[s * 128:(s + 1) * 128] = np.arange(bl * 128, (bl + 1) * 128)
        ck_h, sk_h = _rope_tables(pos)
        m = dict(common)
        m["xT"] = xTc
        m["xsw"] = np.ascontiguousarray(
            xTc.reshape(32, 128, 2, 1024).transpose(2, 1, 0, 3).reshape(2, 128, 32 * 1024))
        m["ck_t"] = ck_h
        m["sk_t"] = sk_h
        m["btbl"] = _bias_tbl(half)
        in_maps.append(m)
    return in_maps


def kernel(**inputs):
    key = ("dbg" if _DBG else "std", _STOP)
    if key not in _CACHE:
        _CACHE[key] = _build_program(dbg=_DBG)
    nc = _CACHE[key]
    in_maps = _prep_inputs(inputs)
    res = run_bass_kernel_spmd(nc, in_maps, core_ids=list(range(8)))
    out = np.zeros((4, 2048, D), np.float32)
    for core in range(8):
        o = res.results[core]["out"]
        b, half = core // 2, core % 2
        for j, bl in enumerate(_own_blocks(half)):
            out[b, bl * 128:(bl + 1) * 128, :] = o[:, j * 128:(j + 1) * 128].T
    if _DBG:
        kernel._dbg = [res.results[c] for c in range(8)]
    return out
